# version 36
# baseline (speedup 1.0000x reference)
import contextlib
import numpy as np
import concourse.bass as bass
import concourse.mybir as mybir
from concourse.bass_utils import run_bass_kernel_spmd

F32 = mybir.dt.float32
BF16 = mybir.dt.bfloat16
AF = mybir.ActivationFunctionType
ALU = mybir.AluOpType
AX = mybir.AxisListType

D = 1024
T = 2048
NH = 16
HP = 64
NKT = 16

N_SUB = 4
FP64_SAFE = False
DBG = False

ENGS = ("sync", "gpsimd", "tensor", "vector", "scalar")


class Sem:
    def __init__(self, h):
        self.h = h
        self.n = 0

    def tick(self, k=1):
        self.n += k
        return (self, self.n, k)

    def now(self):
        return (self, self.n, 0)


class Em:

    def __init__(self):
        self.q = {e: [] for e in ENGS}
        self.last_dma = {}

    def op(self, eng, meth, *args, wait=(), inc=None, **kw):
        if FP64_SAFE and meth == "dma_start" and inc is not None:
            prev = self.last_dma.get(id(inc[0]))
            if prev is not None:
                wait = tuple(wait) + (prev,)
            self.last_dma[id(inc[0])] = inc
        waits = tuple((t[0], t[1]) for t in wait if t is not None and t[1] > 0)

        def f(e, seen, meth=meth, args=args, kw=kw, waits=waits, inc=inc):
            for (s, v) in waits:
                if seen.get(id(s), 0) < v:
                    e.wait_ge(s.h, v)
                    seen[id(s)] = v
            ins = getattr(e, meth)(*args, **kw)
            if inc is not None:
                ins.then_inc(inc[0].h, inc[2])
        self.q[eng].append(f)

    def wait(self, eng, tok):
        if tok is None or tok[1] <= 0:
            return

        def f(e, seen, s=tok[0], v=tok[1]):
            if seen.get(id(s), 0) < v:
                e.wait_ge(s.h, v)
                seen[id(s)] = v
        self.q[eng].append(f)


PHASE_MARKS = []


def build_nc():
    nc = bass.Bass("TRN2", target_bir_lowering=False,
                   dynamic_dma_scratch_size=16384)

    xt = nc.dram_tensor("xt", [D, T], F32, kind="ExternalInput")
    y = nc.dram_tensor("y", [D, T], F32, kind="ExternalOutput")
    wdr = {}
    for nm in ("aWq", "aWk", "aWv", "fWr", "fWk", "fWv", "bQw", "bKw", "bVw", "bLw"):
        wdr[nm] = nc.dram_tensor(nm, [D, D], F32, kind="ExternalInput")
    lb_d = nc.dram_tensor("lb", [128, 8], F32, kind="ExternalInput")
    mur_d = nc.dram_tensor("mur", [128, 8], F32, kind="ExternalInput")
    muk_d = nc.dram_tensor("muk", [128, 8], F32, kind="ExternalInput")
    vbr_d = nc.dram_tensor("vbr", [1, D], F32, kind="ExternalInput")
    qbr_d = nc.dram_tensor("qbr", [1, D], F32, kind="ExternalInput")
    kbr_d = nc.dram_tensor("kbr", [1, D], F32, kind="ExternalInput")
    vdram = nc.dram_tensor("vdram", [T, D], BF16, kind="Internal")

    X = nc.alloc_sbuf_tensor("X", [128, 8, T], F32)
    XN = nc.alloc_sbuf_tensor("XN", [128, 8, T], BF16)
    B1 = nc.alloc_sbuf_tensor("B1", [128, 8, T], BF16)
    B2 = nc.alloc_sbuf_tensor("B2", [128, 8, T], BF16)
    WBUF = nc.alloc_sbuf_tensor("WBUF", [128, 3, 8, 128], BF16)
    KTP = nc.alloc_sbuf_tensor("KTP", [128, 2, 2048], BF16)
    EXPS = nc.alloc_sbuf_tensor("EXPS", [128, 3, T], BF16)
    VSTG = nc.alloc_sbuf_tensor("VSTG", [128, 10, 512], BF16)
    VRING = nc.alloc_sbuf_tensor("VRING", [128, 6, HP], BF16)
    VSC = nc.alloc_sbuf_tensor("VSC", [128, 6, 128], BF16)
    SCR = nc.alloc_sbuf_tensor("SCR", [128, 512], F32)
    ONESR = nc.alloc_sbuf_tensor("ONESR", [1, 512], BF16)
    LB = nc.alloc_sbuf_tensor("LB", [128, 8], F32)
    MUR = nc.alloc_sbuf_tensor("MUR", [128, 8], F32)
    MUK = nc.alloc_sbuf_tensor("MUK", [128, 8], F32)
    GR = nc.alloc_sbuf_tensor("GR", [128, 8], F32)
    ZH = nc.alloc_sbuf_tensor("ZH", [128, 4, 4], F32)
    ZS = nc.alloc_sbuf_tensor("ZS", [128, 4], F32)
    RZ = nc.alloc_sbuf_tensor("RZ", [128, 4], F32)
    CM1 = nc.alloc_sbuf_tensor("CM1", [128, 1], BF16)
    C1B = nc.alloc_sbuf_tensor("C1B", [128, 1], BF16)
    C1F = nc.alloc_sbuf_tensor("C1F", [128, 1], F32)
    CC = nc.alloc_sbuf_tensor("CC", [128, 1], F32)
    BN16 = nc.alloc_sbuf_tensor("BN16", [128, 1], F32)
    EPSC = nc.alloc_sbuf_tensor("EPSC", [1, 8], F32)
    ONESP = nc.alloc_sbuf_tensor("ONESP", [128, 128], BF16)

    X2 = EXPS[:, 0, :]
    ROWAf = EXPS[0:1, 0, :]
    ROWBf = EXPS[0:1, 2, :]
    ROWA = EXPS[0:1, 1, 0:512]
    ROWB = EXPS[0:1, 1, 512:1024]
    VSTGf = VSTG.bitcast(F32).reshape([128, 2560])
    SCRB = SCR.bitcast(BF16).reshape([128, 1024])
    JNK = SCRB
    ROW2f = VSTGf[0:1, 0:2048]
    ROW3 = VSTGf[0:1, 2048:2560]
    WBUFf = WBUF.reshape([128, 3072])
    VBR = KTP[0:1, 0, 0:1024]
    QBR = KTP[0:1, 0, 1024:2048]
    KBR = KTP[0:1, 1, 0:1024]

    PA = nc.alloc_psum_tensor("PA", [128, T], F32)
    PB = nc.alloc_psum_tensor("PB", [128, T], F32)

    em = Em()
    stack = contextlib.ExitStack()
    sQ = Sem(stack.enter_context(nc.semaphore("sQ")))
    sG = Sem(stack.enter_context(nc.semaphore("sG")))
    sT = Sem(stack.enter_context(nc.semaphore("sT")))
    sV = Sem(stack.enter_context(nc.semaphore("sV")))
    sS = Sem(stack.enter_context(nc.semaphore("sS")))
    sR = Sem(stack.enter_context(nc.semaphore("sR")))

    def pslot_ap(i):
        base = PA if i < 4 else PB
        q = i % 4
        return base[:, q * 512:(q + 1) * 512]
    ps_read = [None] * 8

    wslot_free = [None, None, None]
    wring = [0]

    def proj(wname, rhs_tensor, evac, rhs_ready=None, rhs_q=None,
             stationary_extra=None, extra_wait=None):
        wd = wdr[wname]
        wap = wd.rearrange("(dc p) (ec j) -> p dc ec j", p=128, j=128)
        ec_last_tok = {}
        last_tok = None
        for ec in range(8):
            s = wring[0] % 3
            wring[0] += 1
            ltw = [wslot_free[s]]
            if extra_wait is not None and ec == 0:
                ltw.append(extra_wait)
            ltok = sG.tick(16)
            em.op("gpsimd", "dma_start",
                  out=WBUF[:, s, :, :], in_=wap[:, :, ec, :],
                  wait=ltw, inc=ltok)
            for tq in range(4):
                pi = (ec * 4 + tq) % 8
                pt = pslot_ap(pi)
                for dc in range(8):
                    w = []
                    if dc == 0:
                        w = [ps_read[pi], ltok]
                        if rhs_q is not None:
                            w.append(rhs_q[tq])
                        if ec == 0 and tq == 0 and rhs_ready is not None:
                            w.append(rhs_ready)
                    is_last = (dc == 7 and stationary_extra is None)
                    tok = sT.tick() if is_last else None
                    em.op("tensor", "matmul", pt,
                          WBUF[:, s, dc, :],
                          rhs_tensor[:, dc, tq * 512:(tq + 1) * 512],
                          start=(dc == 0), stop=is_last,
                          wait=w, inc=tok)
                if stationary_extra is not None:
                    lhsT_fn, rhs_fn, bias_tok = stationary_extra
                    tok = sT.tick()
                    em.op("tensor", "matmul", pt, lhsT_fn(ec, tq), rhs_fn(ec, tq),
                          start=False, stop=True, wait=(bias_tok, t_const), inc=tok)
                ps_read[pi] = evac(ec, tq, pt, tok)
                last_tok = tok
            wslot_free[s] = last_tok
            ec_last_tok[ec] = last_tok
        return last_tok

    ln_bc_free = [None]

    def layer_norm(x_ready, qhook=None):
        if not isinstance(x_ready, list):
            x_ready = [x_ready] * 8
        cast_toks = []
        for dc in range(8):
            tok = sV.tick()
            em.op("vector", "tensor_copy", XN[:, dc, :], X[:, dc, :],
                  wait=(x_ready[dc],), inc=tok)
            cast_toks.append(tok)
        sq_gate = None
        stat_tok = None
        for dc in range(8):
            sqtok = sS.tick()
            em.op("scalar", "activation", X2, XN[:, dc, :], AF.Square,
                  wait=(cast_toks[dc], sq_gate), inc=sqtok)
            for tq in range(4):
                w_m = [cast_toks[dc]]
                w_s = [sqtok]
                if dc == 0:
                    w_m.append(ps_read[tq])
                    w_s.append(ps_read[4 + tq])
                mt = sT.tick() if tq == 3 else None
                em.op("tensor", "matmul", PA[0:1, tq * 512:(tq + 1) * 512],
                      CM1[:], XN[:, dc, tq * 512:(tq + 1) * 512],
                      start=(dc == 0), stop=(dc == 7), wait=w_m, inc=None)
                em.op("tensor", "matmul", PB[0:1, tq * 512:(tq + 1) * 512],
                      CM1[:], X2[:, tq * 512:(tq + 1) * 512],
                      start=(dc == 0), stop=(dc == 7), wait=w_s, inc=mt)
                if tq == 3:
                    sq_gate = mt
            stat_tok = sq_gate
        xn_tok = None
        xn_q = []
        t_mb = sS.tick()
        em.op("scalar", "activation", ROWAf, PA[0:1, 0:T], AF.Copy,
              wait=(stat_tok,), inc=t_mb)
        t_msq = sS.tick()
        em.op("scalar", "activation", ROW2f, PA[0:1, 0:T], AF.Square,
              wait=(), inc=t_msq)
        t_var = sV.tick()
        em.op("vector", "scalar_tensor_tensor", ROW2f, PB[0:1, 0:T],
              EPSC[0:1, 0:1], ROW2f, ALU.add, ALU.subtract,
              wait=(t_msq, t_const), inc=t_var)
        t_exs = []
        for tq in range(4):
            em.op("scalar", "activation", ROW3,
                  ROW2f[0:1, tq * 512:(tq + 1) * 512], AF.Ln,
                  wait=(t_var,) if tq == 0 else (), inc=None)
            t_ex = sS.tick()
            em.op("scalar", "activation",
                  ROWBf[0:1, tq * 512:(tq + 1) * 512], ROW3, AF.Exp,
                  scale=-0.5, wait=(), inc=t_ex)
            t_exs.append(t_ex)
        for tq in range(4):
            qs = slice(tq * 512, (tq + 1) * 512)
            t_bm = sT.tick()
            em.op("tensor", "matmul", PA[:, qs], ONESP[:],
                  X2[:, qs],
                  start=True, stop=True, wait=(t_msq,), inc=t_bm)
            t_br = sT.tick()
            em.op("tensor", "matmul", PB[:, qs], ONESP[:],
                  EXPS[:, 2, qs],
                  start=True, stop=True, wait=(t_exs[tq],), inc=t_br)
            mb_s = EXPS[:, 1, 1024:1536]
            rb_s = EXPS[:, 1, 1536:2048]
            t_eb = sV.tick()
            em.op("vector", "tensor_copy", mb_s, PA[:, qs],
                  wait=(t_bm, ln_bc_free[0]), inc=None)
            em.op("vector", "tensor_copy", rb_s, PB[:, qs],
                  wait=(t_br,), inc=t_eb)
            for dc in range(8):
                em.op("vector", "tensor_sub", XN[:, dc, qs], XN[:, dc, qs],
                      mb_s, wait=(t_eb,), inc=None)
                tok = sV.tick()
                em.op("vector", "tensor_mul", XN[:, dc, qs], XN[:, dc, qs],
                      rb_s, wait=(), inc=tok)
                xn_tok = tok
            ln_bc_free[0] = xn_tok
            ps_read[tq] = xn_tok
            ps_read[4 + tq] = xn_tok
            if qhook is not None:
                xn_tok = qhook(tq, xn_tok)
            xn_q.append(xn_tok)
        return xn_q

    def sumsq_rows(src, src_toks):
        if not isinstance(src_toks, list):
            src_toks = [src_toks] * 8
        sq_gate = None
        last = None
        for ec in range(8):
            sqtok = sS.tick()
            em.op("scalar", "activation", X2, src[:, ec, :], AF.Square,
                  wait=(src_toks[ec], sq_gate), inc=sqtok)
            for tq in range(4):
                w = [sqtok]
                if ec == 0:
                    w.append(ps_read[tq])
                mt = sT.tick() if tq == 3 else None
                em.op("tensor", "matmul", PA[0:1, tq * 512:(tq + 1) * 512],
                      C1B[:], X2[:, tq * 512:(tq + 1) * 512],
                      start=(ec == 0), stop=(ec == 7), wait=w, inc=mt)
                if tq == 3:
                    sq_gate = mt
            last = sq_gate
        return last

    def rs_row(tq, ss_tok, prev_b):
        qs = slice(tq * 512, (tq + 1) * 512)
        em.op("scalar", "activation", ROW3, PA[0:1, qs], AF.Ln,
              wait=(ss_tok,), inc=None)
        t = sS.tick()
        em.op("scalar", "activation", ROWB, ROW3, AF.Exp, scale=-0.5,
              wait=(prev_b,), inc=t)
        return t

    x_toks = []
    for dc in range(8):
        if dc % 2 == 0:
            tok = sQ.tick(16)
            em.op("sync", "dma_start", out=X[:, dc, :],
                  in_=xt[dc * 128:(dc + 1) * 128, :], wait=(), inc=tok)
        else:
            tok = sR.tick(16)
            em.op("sync", "dma_start", out=X[:, dc, :],
                  in_=xt[dc * 128:(dc + 1) * 128, :], wait=(), inc=tok)
        x_toks.append(tok)
    em.op("sync", "dma_start", out=LB[:], in_=lb_d[:], wait=(), inc=sQ.tick(16))
    em.op("sync", "dma_start", out=MUR[:], in_=mur_d[:], wait=(), inc=sQ.tick(16))
    t_small = sQ.tick(16)
    em.op("sync", "dma_start", out=MUK[:], in_=muk_d[:], wait=(), inc=t_small)
    em.op("gpsimd", "dma_start", out=VBR[:], in_=vbr_d[:], wait=(), inc=sG.tick(16))
    em.op("gpsimd", "dma_start", out=QBR[:], in_=qbr_d[:], wait=(), inc=sG.tick(16))
    t_vbr = sG.tick(16)
    em.op("gpsimd", "dma_start", out=KBR[:], in_=kbr_d[:], wait=(), inc=t_vbr)
    t_const = sV.tick()
    em.op("vector", "memset", CM1[:], 1.0 / 1024.0, wait=(), inc=None)
    em.op("vector", "memset", C1B[:], 1.0, wait=(), inc=None)
    em.op("vector", "memset", C1F[:], 1.0, wait=(), inc=None)
    em.op("vector", "memset", CC[:], 1.0 / 2048.0, wait=(), inc=None)
    em.op("vector", "memset", BN16[:], -16.0, wait=(), inc=None)
    em.op("vector", "memset", ONESR[:], 1.0, wait=(), inc=None)
    em.op("vector", "memset", EPSC[:], 1e-5, wait=(), inc=None)
    em.op("vector", "memset", ONESP[:], 0.0, wait=(), inc=None)
    em.op("vector", "memset", ONESP[0:1, :], 1.0, wait=(), inc=None)
    em.op("vector", "memset", EXPS[:, 1, :], 0.0, wait=(), inc=None)
    em.op("vector", "memset", EXPS[:, 2, :], 0.0, wait=(), inc=t_const)

    state = {"x_ready": x_toks}

    def attn_a():
        xn_q = layer_norm(state["x_ready"])

        def ev_v(ec, tq, pt, mmtok):
            tok = sV.tick()
            em.op("vector", "tensor_copy",
                  B1[:, ec, tq * 512:(tq + 1) * 512], pt,
                  wait=(mmtok,), inc=tok)
            return tok
        proj("aWv", XN, ev_v, rhs_q=xn_q)

        k_c = [None] * 8

        def ev_k(ec, tq, pt, mmtok):
            tok = sS.tick()
            em.op("scalar", "activation",
                  B2[:, ec, tq * 512:(tq + 1) * 512], pt, AF.Copy,
                  wait=(mmtok,), inc=tok)
            k_c[ec] = tok
            return tok
        proj("aWk", XN, ev_k, rhs_q=xn_q)
        k_done = k_c[7]

        PHASE_MARKS.append(("attn_a:ssk", sT.n, sV.n, sS.n))
        ss_tok = sumsq_rows(B2, k_c)
        kn_tok = None
        kn_ec = [None] * 8
        prev_b = None
        for tq in range(4):
            qs = slice(tq * 512, (tq + 1) * 512)
            rs_s = (EXPS[:, 1, 1024 + (tq % 2) * 512:1536 + (tq % 2) * 512]
                    if tq < 2 else
                    EXPS[:, 2, (tq - 2) * 512:(tq - 1) * 512])
            t_rs = rs_row(tq, ss_tok, prev_b)
            t_b = sT.tick()
            em.op("tensor", "matmul", PB[:, qs], ONESP[:],
                  EXPS[:, 1, 512:1024],
                  start=True, stop=True, wait=(t_rs, ps_read[4 + tq]), inc=t_b)
            prev_b = t_b
            t_ev = sV.tick()
            em.op("vector", "tensor_copy", rs_s, PB[:, qs],
                  wait=(t_b, ln_bc_free[0]), inc=t_ev)
            for ec in range(8):
                tok = sV.tick()
                em.op("vector", "tensor_mul", B2[:, ec, qs], B2[:, ec, qs],
                      rs_s, wait=(t_ev,), inc=tok)
                kn_tok = tok
                if tq == 3:
                    kn_ec[ec] = tok
            ps_read[4 + tq] = t_ev
            ps_read[tq] = t_rs
        kv_c = [None] * 8
        kv_tok = None
        for ec in range(8):
            tok = sV.tick()
            em.op("vector", "scalar_tensor_tensor", X2,
                  B2[:, ec, :], C1F[:], B1[:, ec, :],
                  ALU.mult, ALU.mult, accum_out=GR[:, ec:ec + 1],
                  wait=(kn_ec[ec],), inc=tok)
            kv_c[ec] = tok
            kv_tok = tok
        gr_tok = sV.tick()
        em.op("vector", "tensor_scalar", GR[:], GR[:], CC[:], None, ALU.mult,
              wait=(kv_tok,), inc=gr_tok)

        q_c = [None] * 8

        def ev_q(ec, tq, pt, mmtok):
            tok = sS.tick()
            em.op("scalar", "activation",
                  B1[:, ec, tq * 512:(tq + 1) * 512], pt, AF.Copy,
                  wait=(mmtok, kv_c[ec]), inc=tok)
            q_c[ec] = tok
            return tok
        proj("aWq", XN, ev_q)

        ss2 = sumsq_rows(B1, q_c)
        fin_c = [None] * 8
        prev_b = None
        rq_s = []
        t_eq = []
        for tq in range(4):
            rs_s = (EXPS[:, 1, 1024 + (tq % 2) * 512:1536 + (tq % 2) * 512]
                    if tq < 2 else
                    EXPS[:, 2, (tq - 2) * 512:(tq - 1) * 512])
            rq_s.append(rs_s)
            t_rs = rs_row(tq, ss2, prev_b)
            t_b = sT.tick()
            em.op("tensor", "matmul", PB[:, tq * 512:(tq + 1) * 512],
                  ONESP[:], EXPS[:, 1, 512:1024],
                  start=True, stop=True, wait=(t_rs, ps_read[4 + tq]), inc=t_b)
            prev_b = t_b
            t_ev = sV.tick()
            em.op("vector", "tensor_copy", rs_s, PB[:, tq * 512:(tq + 1) * 512],
                  wait=(t_b, kn_tok), inc=t_ev)
            t_eq.append(t_ev)
            ps_read[tq] = t_rs
            ps_read[4 + tq] = t_ev
        bcq = SCR.bitcast(BF16).reshape([128, 1024])
        for c in range(8):
            for tq in range(4):
                qs = slice(tq * 512, (tq + 1) * 512)
                em.op("vector", "scalar_tensor_tensor", bcq[:, 0:512],
                      B1[:, c, qs], GR[:, c:c + 1], rq_s[tq],
                      ALU.mult, ALU.mult, wait=(t_eq[tq], gr_tok), inc=None)
                tok = sV.tick()
                em.op("vector", "tensor_add", X[:, c, qs], X[:, c, qs],
                      bcq[:, 0:512], wait=(), inc=tok)
            fin_c[c] = tok
        ln_bc_free[0] = fin_c[7]
        state["x_ready"] = fin_c

    def ffn():
        def mix_hook(tq, tok):
            lo = tq * 512
            hi = (tq + 1) * 512
            s = 1 if tq == 0 else 0
            for dc in range(8):
                sh1 = XN[:, dc, lo + s:hi]
                sh0 = XN[:, dc, lo + s - 1:hi - 1]
                em.op("vector", "tensor_sub", X2[:, lo + s:hi], sh1, sh0,
                      wait=(tok,), inc=None)
                em.op("vector", "scalar_tensor_tensor", B1[:, dc, lo + s:hi],
                      X2[:, lo + s:hi], MUR[:, dc:dc + 1], sh0,
                      ALU.mult, ALU.add, wait=(), inc=None)
                t2 = sV.tick()
                em.op("vector", "scalar_tensor_tensor", B2[:, dc, lo + s:hi],
                      X2[:, lo + s:hi], MUK[:, dc:dc + 1], sh0,
                      ALU.mult, ALU.add, wait=(), inc=t2)
                if tq == 0:
                    em.op("vector", "tensor_scalar", B1[:, dc, 0:1],
                          XN[:, dc, 0:1], MUR[:, dc:dc + 1], None, ALU.mult,
                          wait=(), inc=None)
                    t2 = sV.tick()
                    em.op("vector", "tensor_scalar", B2[:, dc, 0:1],
                          XN[:, dc, 0:1], MUK[:, dc:dc + 1], None, ALU.mult,
                          wait=(), inc=t2)
            return t2

        mix_q = layer_norm(state["x_ready"], qhook=mix_hook)

        def ev_r(ec, tq, pt, mmtok):
            tok = sS.tick()
            em.op("scalar", "activation",
                  XN[:, ec, tq * 512:(tq + 1) * 512], pt, AF.Sigmoid,
                  wait=(mmtok,), inc=tok)
            return tok
        PHASE_MARKS.append(("ffn:rp", sT.n, sV.n, sS.n))
        r_last_mm = proj("fWr", B1, ev_r, rhs_q=mix_q)

        prev_sq = [None]
        kk_q = [None] * 4

        def ev_kk(ec, tq, pt, mmtok):
            qs = slice(tq * 512, (tq + 1) * 512)
            t1 = sS.tick()
            em.op("scalar", "activation", X2[:, qs], pt, AF.Relu,
                  wait=(mmtok, prev_sq[0]), inc=t1)
            t2 = sV.tick()
            em.op("vector", "tensor_mul", B1[:, ec, qs], X2[:, qs], X2[:, qs],
                  wait=(t1, r_last_mm), inc=t2)
            prev_sq[0] = t2
            if ec == 7:
                kk_q[tq] = t2
            return t2
        proj("fWk", B2, ev_kk, rhs_q=mix_q)
        PHASE_MARKS.append(("ffn:vp", sT.n, sV.n, sS.n))

        fin_c = [None] * 8

        def ev_vv(ec, tq, pt, mmtok):
            qs = slice(tq * 512, (tq + 1) * 512)
            em.op("vector", "scalar_tensor_tensor", SCR[:], pt,
                  C1F[:], XN[:, ec, qs], ALU.mult, ALU.mult,
                  wait=(mmtok,), inc=None)
            tok = sV.tick()
            em.op("vector", "tensor_add", X[:, ec, qs], X[:, ec, qs], SCR[:],
                  wait=(), inc=tok)
            fin_c[ec] = tok
            return tok
        proj("fWv", B1, ev_vv, rhs_q=kk_q)
        state["x_ready"] = fin_c

    def attn_b():
        xn_q = layer_norm(state["x_ready"])
        xn_tok = xn_q[3]

        PHASE_MARKS.append(("attn_b:vproj", sT.n, sV.n, sS.n))
        wv = wdr["bVw"]
        vgate = sT.now()
        mm_toks = {}
        vst_free = [None] * 4
        vd_tok = None
        nstg = 0
        for grp in range(2):
            for hpq in range(2):
                hs = slice(hpq * 512, (hpq + 1) * 512)
                pts = list(range(8))
                for dc in range(8):
                    s = dc % 4
                    g = grp * 2 + hpq
                    lw = [mm_toks.get((g, dc - 4)) if dc >= 4 else vgate]
                    ltok = sG.tick(16)
                    em.op("gpsimd", "dma_start",
                          out=WBUFf[:, s * 512:(s + 1) * 512],
                          in_=wv[dc * 128:(dc + 1) * 128, hs],
                          wait=lw, inc=ltok)
                    for i, sc in enumerate(range(grp * 8, grp * 8 + 8)):
                        pi = pts[i]
                        w = [ltok]
                        if dc == 0:
                            w.append(ps_read[pi])
                            if hpq == 0 and i == 0:
                                w.append(xn_q[grp * 2 + (i >= 4)])
                        tok = sT.tick() if i == 7 else None
                        em.op("tensor", "matmul", pslot_ap(pi),
                              XN[:, dc, sc * 128:(sc + 1) * 128],
                              WBUFf[:, s * 512:(s + 1) * 512],
                              start=(dc == 0), stop=False,
                              wait=w, inc=tok)
                    mm_toks[(g, dc)] = tok
                vgate = mm_toks[(g, 7)]
                for i, sc in enumerate(range(grp * 8, grp * 8 + 8)):
                    pi = pts[i]
                    mt = sT.tick()
                    em.op("tensor", "matmul", pslot_ap(pi), ONESR[0:1, 0:128],
                          VBR[0:1, hs], start=False, stop=True,
                          wait=(t_vbr, t_const), inc=mt)
                    vgate = mt
                    par = nstg % 4
                    nstg += 1
                    vt = sV.tick()
                    em.op("vector", "tensor_copy", VSTG[:, 6 + par, :],
                          pslot_ap(pi), wait=(mt, vst_free[par]), inc=vt)
                    ps_read[pi] = vt
                    dt = sR.tick(16)
                    em.op("sync", "dma_start",
                          out=vdram[sc * 128:(sc + 1) * 128, hs],
                          in_=VSTG[:, 6 + par, :], wait=(vt,), inc=dt)
                    vst_free[par] = dt
                    vd_tok = dt
                wslot_free[0] = vgate
        wslot_free[1] = vgate
        v_done = vd_tok

        PHASE_MARKS.append(("attn_b:loop", sT.n, sV.n, sS.n))
        def ev_bq(ec, tq, pt, mmtok):
            tok = sS.tick()
            em.op("scalar", "activation",
                  B1[:, ec, tq * 512:(tq + 1) * 512], pt, AF.Copy,
                  wait=(mmtok,), inc=tok)
            return tok
        proj("bQw", XN, ev_bq, rhs_q=xn_q,
             stationary_extra=(
                 lambda ec, tq: QBR[0:1, ec * 128:(ec + 1) * 128],
                 lambda ec, tq: ONESR[0:1, 0:512], t_vbr))

        def ev_bk(ec, tq, pt, mmtok):
            tok = sS.tick()
            em.op("scalar", "activation",
                  B2[:, ec, tq * 512:(tq + 1) * 512], pt, AF.Copy,
                  wait=(mmtok,), inc=tok)
            return tok
        proj("bKw", XN, ev_bk, rhs_q=xn_q,
             stationary_extra=(
                 lambda ec, tq: KBR[0:1, ec * 128:(ec + 1) * 128],
                 lambda ec, tq: ONESR[0:1, 0:512], t_vbr))

        first_w = [t for t in ps_read[0:4] if t is not None]
        pb_free = [t for t in ps_read[4:8] if t is not None]
        kb_done = sS.now()
        exp_tok = {}
        sc_tok = {}
        ctx_tok = {}
        vsc_tok = {}
        z_tok = {}
        sc_last = {}
        ktp_tok = {}
        evac = {"tok": None}
        NIT = NH * NKT

        def prep_ktp(h):
            half_p = (h % 2) * 64
            s = h % 2
            w = [kb_done, sc_last.get(h - 2)]
            em.op("vector", "memset", KTP[64 - half_p:128 - half_p, s, :], 0.0,
                  wait=w, inc=None)
            tok = sV.tick()
            em.op("vector", "tensor_copy", KTP[half_p:half_p + 64, s, :],
                  B2[half_p:half_p + 64, h // 2, :], wait=(), inc=tok)
            ktp_tok[h] = tok

        def emit_scores(it, qh):
            par3 = it % 3
            h = it // NKT
            kt = it % NKT
            chunk = h // 2
            if qh == 0:
                if kt == 0:
                    if h == 0:
                        prep_ktp(0)
                    if h + 1 < NH:
                        prep_ktp(h + 1)
                vr = sR.tick(16)
                em.op("sync", "dma_start", out=VRING[:, it % 6, :],
                      in_=vdram[kt * 128:(kt + 1) * 128,
                                h * HP:(h + 1) * HP],
                      wait=(v_done, vsc_tok.get(it - 6)), inc=vr)
                sc_state[it] = {"vr": vr}
            for qq in (2 * qh, 2 * qh + 1):
                w = [exp_tok.get((it - 1, qh))]
                if qq == 2 * qh:
                    w.append(ktp_tok[h])
                    if it == 0:
                        w += first_w
                tok = sT.tick() if qq == 2 * qh + 1 else None
                em.op("tensor", "matmul", PA[:, qq * 512:(qq + 1) * 512],
                      KTP[:, h % 2, kt * 128:(kt + 1) * 128],
                      B1[:, chunk, qq * 512:(qq + 1) * 512],
                      start=True, stop=True, wait=w, inc=tok)
            sc_tok[(it, qh)] = tok
            hs = slice(qh * 1024, (qh + 1) * 1024)
            et = sS.tick()
            em.op("scalar", "activation",
                  EXPS[:, par3, hs], PA[:, hs],
                  AF.Exp, bias=BN16[:], scale=0.125,
                  wait=(tok, t_const, ctx_tok.get(it - 3)), inc=et)
            exp_tok[(it, qh)] = et
            if qh == 1:
                sc_last[h] = tok

        def emit_scores_tail(it):
            par = it % 4
            par3 = it % 3
            h = it // NKT
            kt = it % NKT
            half_p = (h % 2) * 64
            vr = sc_state.pop(it)["vr"]
            t_za = sV.tick()
            em.op("vector", "scalar_tensor_tensor", JNK[:],
                  EXPS[:, par3, 0:1024], C1F[:], EXPS[:, par3, 1024:2048],
                  ALU.mult, ALU.add, accum_out=ZS[:, par:par + 1],
                  wait=(exp_tok[(it, 0)], exp_tok[(it, 1)]), inc=t_za)
            t_rc = sV.tick()
            em.op("vector", "reciprocal", RZ[:, par:par + 1],
                  ZS[:, par:par + 1], wait=(t_za,), inc=t_rc)
            if kt < 6:
                em.op("vector", "memset",
                      VSC[:, it % 6, 64 - half_p:128 - half_p],
                      0.0, wait=(ctx_tok.get(it - 6),), inc=None)
            vt = sV.tick()
            em.op("vector", "tensor_scalar",
                  VSC[:, it % 6, half_p:half_p + 64],
                  VRING[:, it % 6, :], RZ[:, par:par + 1], None, ALU.mult,
                  wait=(vr, t_rc, ctx_tok.get(it - 6)), inc=vt)
            vsc_tok[it] = vt

        sc_state = {}
        def emit_ctx_half(it, half):
            par = it % 4
            par3 = it % 3
            h = it // NKT
            kt = it % NKT
            chunk = h // 2
            tok = None
            for qq in (2 * half, 2 * half + 1):
                w = [vsc_tok[it]]
                if kt == 0 and qq == 0:
                    w.append(evac["tok"])
                    if it <= 1:
                        w += pb_free
                tok = sT.tick() if qq == 3 else None
                em.op("tensor", "matmul",
                      PB[:, qq * 512:(qq + 1) * 512],
                      VSC[:, it % 6, :],
                      EXPS[:, par3, qq * 512:(qq + 1) * 512],
                      start=(kt == 0 and h % 2 == 0),
                      stop=(kt == NKT - 1 and h % 2 == 1),
                      wait=w, inc=tok)
            if qq == 3:
                ctx_tok[it] = tok
                if kt == NKT - 1 and h % 2 == 1:
                    t = sV.tick()
                    em.op("vector", "tensor_copy", XN[:, chunk, :],
                          PB[:, 0:T], wait=(tok,), inc=t)
                    evac["tok"] = t

        for it in range(NIT + 2):
            if it < NIT:
                emit_scores(it, 0)
            if it >= 2:
                emit_ctx_half(it - 2, 0)
            if it < NIT:
                emit_scores(it, 1)
            if it >= 2:
                emit_ctx_half(it - 2, 1)
            if it < NIT:
                emit_scores_tail(it)
        evac_tok = evac["tok"]
        for i in range(8):
            ps_read[i] = exp_tok[(NIT - 1, i // 2)] if i < 4 else evac_tok

        PHASE_MARKS.append(("attn_b:outproj", sT.n, sV.n, sS.n))
        fin_c = [None] * 8

        def ev_o(ec, tq, pt, mmtok):
            qs = slice(tq * 512, (tq + 1) * 512)
            tok = sV.tick()
            em.op("vector", "scalar_tensor_tensor", X[:, ec, qs], pt,
                  LB[:, ec:ec + 1], X[:, ec, qs], ALU.add, ALU.add,
                  wait=(mmtok, t_small), inc=tok)
            fin_c[ec] = tok
            return tok
        proj("bLw", XN, ev_o, rhs_ready=evac_tok)
        state["x_ready"] = fin_c

    subs = [attn_a, ffn, attn_b, ffn]
    names = ["attn_a", "ffn1", "attn_b", "ffn2"]
    PHASE_MARKS.clear()
    for nm, fsub in zip(names[:N_SUB], subs[:N_SUB]):
        PHASE_MARKS.append((nm + ":start", sT.n, sV.n, sS.n))
        fsub()
    PHASE_MARKS.append(("end", sT.n, sV.n, sS.n))

    xr_fin = state["x_ready"]
    if not isinstance(xr_fin, list):
        xr_fin = [xr_fin] * 8
    out_toks = []
    for dc in range(8):
        tok = sQ.tick(16)
        em.op("sync", "dma_start", out=y[dc * 128:(dc + 1) * 128, :],
              in_=X[:, dc, :], wait=(xr_fin[dc],), inc=tok)
        out_toks.append(tok)
    if DBG:
        dXN = nc.dram_tensor("dXN", [D, T], BF16, kind="ExternalOutput")
        dB1 = nc.dram_tensor("dB1", [D, T], BF16, kind="ExternalOutput")
        dB2 = nc.dram_tensor("dB2", [D, T], BF16, kind="ExternalOutput")
        dGR = nc.dram_tensor("dGR", [128, 8], F32, kind="ExternalOutput")
        for dc in range(8):
            em.op("sync", "dma_start", out=dXN[dc * 128:(dc + 1) * 128, :],
                  in_=XN[:, dc, :], wait=(state["x_ready"],), inc=sQ.tick(16))
            em.op("sync", "dma_start", out=dB1[dc * 128:(dc + 1) * 128, :],
                  in_=B1[:, dc, :], wait=(), inc=sQ.tick(16))
            em.op("sync", "dma_start", out=dB2[dc * 128:(dc + 1) * 128, :],
                  in_=B2[:, dc, :], wait=(), inc=sQ.tick(16))
        out_toks.append(sQ.tick(16))
        em.op("sync", "dma_start", out=dGR[:], in_=GR[:], wait=(),
              inc=out_toks[-1])
    em.wait("sync", out_toks[-1])

    with nc.Block() as block:
        @block.sync
        def _(e):
            seen = {}
            for f in em.q["sync"]:
                f(e, seen)

        @block.gpsimd
        def _(e):
            seen = {}
            for f in em.q["gpsimd"]:
                f(e, seen)

        @block.tensor
        def _(e):
            seen = {}
            for f in em.q["tensor"]:
                f(e, seen)

        @block.vector
        def _(e):
            seen = {}
            for f in em.q["vector"]:
                f(e, seen)

        @block.scalar
        def _(e):
            seen = {}
            for f in em.q["scalar"]:
                f(e, seen)

    stack.close()
    return nc


_NC_CACHE = {}


def _get_nc():
    if "nc" not in _NC_CACHE:
        _NC_CACHE["nc"] = build_nc()
    return _NC_CACHE["nc"]


def kernel(**inputs):
    f = lambda k: np.ascontiguousarray(np.asarray(inputs[k], dtype=np.float32))
    x = f("L0_attn_a_x")
    tr = lambda a: np.ascontiguousarray(a.T)

    wm = {
        "aWq": tr(f("L0_attn_a_W_q")),
        "aWk": tr(f("L0_attn_a_W_k")),
        "aWv": tr(f("L0_attn_a_W_v")),
        "fWr": tr(f("L0_ffn_W_r")),
        "fWk": tr(f("L0_ffn_W_k")),
        "fWv": tr(f("L0_ffn_W_v")),
        "bQw": tr(f("L0_attn_b_q_w").reshape(D, D)),
        "bKw": tr(f("L0_attn_b_k_w").reshape(D, D)),
        "bVw": tr(f("L0_attn_b_v_w").reshape(D, D)),
        "bLw": tr(f("L0_attn_b_l_w")),
        "qbr": f("L0_attn_b_q_b").reshape(1, D),
        "kbr": f("L0_attn_b_k_b").reshape(1, D),
        "lb": np.ascontiguousarray(f("L0_attn_b_l_b").reshape(8, 128).T),
        "mur": np.ascontiguousarray(f("L0_ffn_mu_r").reshape(8, 128).T),
        "muk": np.ascontiguousarray(f("L0_ffn_mu_k").reshape(8, 128).T),
        "vbr": f("L0_attn_b_v_b").reshape(1, D),
    }

    nc = _get_nc()
    in_maps = []
    for b in range(8):
        m = dict(wm)
        m["xt"] = np.ascontiguousarray(x[b].T)
        in_maps.append(m)
    res = run_bass_kernel_spmd(nc, in_maps, core_ids=list(range(8)))
    out = np.stack([np.asarray(res.results[b]["y"]).T for b in range(8)])
    return np.ascontiguousarray(out.astype(np.float32))


# version 39
# speedup vs baseline: 1.0131x; 1.0131x over previous
import contextlib
import numpy as np
import concourse.bass as bass
import concourse.mybir as mybir
from concourse.bass_utils import run_bass_kernel_spmd

F32 = mybir.dt.float32
BF16 = mybir.dt.bfloat16
AF = mybir.ActivationFunctionType
ALU = mybir.AluOpType
AX = mybir.AxisListType

D = 1024
T = 2048
NH = 16
HP = 64
NKT = 16

N_SUB = 4
FP64_SAFE = False
DBG = False

ENGS = ("sync", "gpsimd", "tensor", "vector", "scalar")


class Sem:
    def __init__(self, h):
        self.h = h
        self.n = 0

    def tick(self, k=1):
        self.n += k
        return (self, self.n, k)

    def now(self):
        return (self, self.n, 0)


class Em:

    def __init__(self):
        self.q = {e: [] for e in ENGS}
        self.last_dma = {}

    def op(self, eng, meth, *args, wait=(), inc=None, **kw):
        if FP64_SAFE and meth == "dma_start" and inc is not None:
            prev = self.last_dma.get(id(inc[0]))
            if prev is not None:
                wait = tuple(wait) + (prev,)
            self.last_dma[id(inc[0])] = inc
        waits = tuple((t[0], t[1]) for t in wait if t is not None and t[1] > 0)

        def f(e, seen, meth=meth, args=args, kw=kw, waits=waits, inc=inc):
            for (s, v) in waits:
                if seen.get(id(s), 0) < v:
                    e.wait_ge(s.h, v)
                    seen[id(s)] = v
            ins = getattr(e, meth)(*args, **kw)
            if inc is not None:
                ins.then_inc(inc[0].h, inc[2])
        self.q[eng].append(f)

    def wait(self, eng, tok):
        if tok is None or tok[1] <= 0:
            return

        def f(e, seen, s=tok[0], v=tok[1]):
            if seen.get(id(s), 0) < v:
                e.wait_ge(s.h, v)
                seen[id(s)] = v
        self.q[eng].append(f)


PHASE_MARKS = []


def build_nc():
    nc = bass.Bass("TRN2", target_bir_lowering=False,
                   dynamic_dma_scratch_size=16384)

    xt = nc.dram_tensor("xt", [D, T], F32, kind="ExternalInput")
    y = nc.dram_tensor("y", [D, T], F32, kind="ExternalOutput")
    wdr = {}
    for nm in ("aWq", "aWk", "aWv", "fWr", "fWk", "fWv", "bQw", "bKw", "bVw", "bLw"):
        wdr[nm] = nc.dram_tensor(nm, [D, D], F32, kind="ExternalInput")
    lb_d = nc.dram_tensor("lb", [128, 8], F32, kind="ExternalInput")
    mur_d = nc.dram_tensor("mur", [128, 8], F32, kind="ExternalInput")
    muk_d = nc.dram_tensor("muk", [128, 8], F32, kind="ExternalInput")
    vbr_d = nc.dram_tensor("vbr", [1, D], F32, kind="ExternalInput")
    qbr_d = nc.dram_tensor("qbr", [1, D], F32, kind="ExternalInput")
    kbr_d = nc.dram_tensor("kbr", [1, D], F32, kind="ExternalInput")
    vdram = nc.dram_tensor("vdram", [T, D], BF16, kind="Internal")

    X = nc.alloc_sbuf_tensor("X", [128, 8, T], F32)
    XN = nc.alloc_sbuf_tensor("XN", [128, 8, T], BF16)
    B1 = nc.alloc_sbuf_tensor("B1", [128, 8, T], BF16)
    B2 = nc.alloc_sbuf_tensor("B2", [128, 8, T], BF16)
    WBUF = nc.alloc_sbuf_tensor("WBUF", [128, 3, 8, 128], BF16)
    KTP = nc.alloc_sbuf_tensor("KTP", [128, 2, 2048], BF16)
    EXPS = nc.alloc_sbuf_tensor("EXPS", [128, 3, T], BF16)
    VSTG = nc.alloc_sbuf_tensor("VSTG", [128, 10, 512], BF16)
    VRING = nc.alloc_sbuf_tensor("VRING", [128, 6, HP], BF16)
    VSC = nc.alloc_sbuf_tensor("VSC", [128, 6, 128], BF16)
    SCR = nc.alloc_sbuf_tensor("SCR", [128, 512], F32)
    ONESR = nc.alloc_sbuf_tensor("ONESR", [1, 512], BF16)
    LB = nc.alloc_sbuf_tensor("LB", [128, 8], F32)
    MUR = nc.alloc_sbuf_tensor("MUR", [128, 8], F32)
    MUK = nc.alloc_sbuf_tensor("MUK", [128, 8], F32)
    GR = nc.alloc_sbuf_tensor("GR", [128, 8], F32)
    ZH = nc.alloc_sbuf_tensor("ZH", [128, 4, 4], F32)
    ZS = nc.alloc_sbuf_tensor("ZS", [128, 4], F32)
    RZ = nc.alloc_sbuf_tensor("RZ", [128, 4], F32)
    CM1 = nc.alloc_sbuf_tensor("CM1", [128, 1], BF16)
    C1B = nc.alloc_sbuf_tensor("C1B", [128, 1], BF16)
    C1F = nc.alloc_sbuf_tensor("C1F", [128, 1], F32)
    CC = nc.alloc_sbuf_tensor("CC", [128, 1], F32)
    BN16 = nc.alloc_sbuf_tensor("BN16", [128, 1], F32)
    EPSC = nc.alloc_sbuf_tensor("EPSC", [1, 8], F32)
    ONESP = nc.alloc_sbuf_tensor("ONESP", [128, 128], BF16)

    X2 = EXPS[:, 0, :]
    ROWAf = EXPS[0:1, 0, :]
    ROWBf = EXPS[0:1, 2, :]
    ROWA = EXPS[0:1, 1, 0:512]
    ROWB = EXPS[0:1, 1, 512:1024]
    VSTGf = VSTG.bitcast(F32).reshape([128, 2560])
    SCRB = SCR.bitcast(BF16).reshape([128, 1024])
    JNK = SCRB
    ROW2f = VSTGf[0:1, 0:2048]
    ROW3 = VSTGf[0:1, 2048:2560]
    WBUFf = WBUF.reshape([128, 3072])
    VBR = KTP[0:1, 0, 0:1024]
    QBR = KTP[0:1, 0, 1024:2048]
    KBR = KTP[0:1, 1, 0:1024]

    PA = nc.alloc_psum_tensor("PA", [128, T], F32)
    PB = nc.alloc_psum_tensor("PB", [128, T], F32)

    em = Em()
    stack = contextlib.ExitStack()
    sQ = Sem(stack.enter_context(nc.semaphore("sQ")))
    sG = Sem(stack.enter_context(nc.semaphore("sG")))
    sT = Sem(stack.enter_context(nc.semaphore("sT")))
    sV = Sem(stack.enter_context(nc.semaphore("sV")))
    sS = Sem(stack.enter_context(nc.semaphore("sS")))
    sR = Sem(stack.enter_context(nc.semaphore("sR")))

    def pslot_ap(i):
        base = PA if i < 4 else PB
        q = i % 4
        return base[:, q * 512:(q + 1) * 512]
    ps_read = [None] * 8

    wslot_free = [None, None, None]
    wring = [0]

    def proj(wname, rhs_tensor, evac, rhs_ready=None, rhs_q=None,
             stationary_extra=None, extra_wait=None, pre_ec=None):
        wd = wdr[wname]
        wap = wd.rearrange("(dc p) (ec j) -> p dc ec j", p=128, j=128)
        ec_last_tok = {}
        last_tok = None
        for ec in range(8):
            if pre_ec is not None:
                pre_ec(ec)
            s = wring[0] % 3
            wring[0] += 1
            ltw = [wslot_free[s]]
            if extra_wait is not None and ec == 0:
                ltw.append(extra_wait)
            ltok = sG.tick(16)
            em.op("gpsimd", "dma_start",
                  out=WBUF[:, s, :, :], in_=wap[:, :, ec, :],
                  wait=ltw, inc=ltok)
            for tq in range(4):
                pi = (ec * 4 + tq) % 8
                pt = pslot_ap(pi)
                for dc in range(8):
                    w = []
                    if dc == 0:
                        w = [ps_read[pi], ltok]
                        if rhs_q is not None:
                            w.append(rhs_q[tq])
                        if ec == 0 and tq == 0 and rhs_ready is not None:
                            w.append(rhs_ready)
                    is_last = (dc == 7 and stationary_extra is None)
                    tok = sT.tick() if is_last else None
                    em.op("tensor", "matmul", pt,
                          WBUF[:, s, dc, :],
                          rhs_tensor[:, dc, tq * 512:(tq + 1) * 512],
                          start=(dc == 0), stop=is_last,
                          wait=w, inc=tok)
                if stationary_extra is not None:
                    lhsT_fn, rhs_fn, bias_tok = stationary_extra
                    tok = sT.tick()
                    em.op("tensor", "matmul", pt, lhsT_fn(ec, tq), rhs_fn(ec, tq),
                          start=False, stop=True, wait=(bias_tok, t_const), inc=tok)
                ps_read[pi] = evac(ec, tq, pt, tok)
                last_tok = tok
            wslot_free[s] = last_tok
            ec_last_tok[ec] = last_tok
        return last_tok

    ln_bc_free = [None]

    def layer_norm(x_ready, qhook=None):
        if not isinstance(x_ready, list):
            x_ready = [x_ready] * 8
        cast_toks = []
        for dc in range(8):
            tok = sV.tick()
            em.op("vector", "tensor_copy", XN[:, dc, :], X[:, dc, :],
                  wait=(x_ready[dc],), inc=tok)
            cast_toks.append(tok)
        sq_gate = None
        stat_tok = None
        for dc in range(8):
            sqtok = sS.tick()
            em.op("scalar", "activation", X2, XN[:, dc, :], AF.Square,
                  wait=(cast_toks[dc], sq_gate), inc=sqtok)
            for tq in range(4):
                w_m = [cast_toks[dc]]
                w_s = [sqtok]
                if dc == 0:
                    w_m.append(ps_read[tq])
                    w_s.append(ps_read[4 + tq])
                mt = sT.tick() if tq == 3 else None
                em.op("tensor", "matmul", PA[0:1, tq * 512:(tq + 1) * 512],
                      CM1[:], XN[:, dc, tq * 512:(tq + 1) * 512],
                      start=(dc == 0), stop=(dc == 7), wait=w_m, inc=None)
                em.op("tensor", "matmul", PB[0:1, tq * 512:(tq + 1) * 512],
                      CM1[:], X2[:, tq * 512:(tq + 1) * 512],
                      start=(dc == 0), stop=(dc == 7), wait=w_s, inc=mt)
                if tq == 3:
                    sq_gate = mt
            stat_tok = sq_gate
        xn_tok = None
        xn_q = []
        t_mb = sS.tick()
        em.op("scalar", "activation", ROWAf, PA[0:1, 0:T], AF.Copy,
              wait=(stat_tok,), inc=t_mb)
        t_msq = sS.tick()
        em.op("scalar", "activation", ROW2f, PA[0:1, 0:T], AF.Square,
              wait=(), inc=t_msq)
        t_var = sV.tick()
        em.op("vector", "scalar_tensor_tensor", ROW2f, PB[0:1, 0:T],
              EPSC[0:1, 0:1], ROW2f, ALU.add, ALU.subtract,
              wait=(t_msq, t_const), inc=t_var)
        t_exs = []
        for tq in range(4):
            em.op("scalar", "activation", ROW3,
                  ROW2f[0:1, tq * 512:(tq + 1) * 512], AF.Ln,
                  wait=(t_var,) if tq == 0 else (), inc=None)
            t_ex = sS.tick()
            em.op("scalar", "activation",
                  ROWBf[0:1, tq * 512:(tq + 1) * 512], ROW3, AF.Exp,
                  scale=-0.5, wait=(), inc=t_ex)
            t_exs.append(t_ex)
        for tq in range(4):
            qs = slice(tq * 512, (tq + 1) * 512)
            t_bm = sT.tick()
            em.op("tensor", "matmul", PA[:, qs], ONESP[:],
                  X2[:, qs],
                  start=True, stop=True, wait=(t_msq,), inc=t_bm)
            t_br = sT.tick()
            em.op("tensor", "matmul", PB[:, qs], ONESP[:],
                  EXPS[:, 2, qs],
                  start=True, stop=True, wait=(t_exs[tq],), inc=t_br)
            mb_s = EXPS[:, 1, 1024:1536]
            rb_s = EXPS[:, 1, 1536:2048]
            t_eb = sV.tick()
            em.op("vector", "tensor_copy", mb_s, PA[:, qs],
                  wait=(t_bm, ln_bc_free[0]), inc=None)
            em.op("vector", "tensor_copy", rb_s, PB[:, qs],
                  wait=(t_br,), inc=t_eb)
            for dc in range(8):
                em.op("vector", "tensor_sub", XN[:, dc, qs], XN[:, dc, qs],
                      mb_s, wait=(t_eb,), inc=None)
                tok = sV.tick()
                em.op("vector", "tensor_mul", XN[:, dc, qs], XN[:, dc, qs],
                      rb_s, wait=(), inc=tok)
                xn_tok = tok
            ln_bc_free[0] = xn_tok
            ps_read[tq] = xn_tok
            ps_read[4 + tq] = xn_tok
            if qhook is not None:
                xn_tok = qhook(tq, xn_tok)
            xn_q.append(xn_tok)
        return xn_q

    def sumsq_rows(src, src_toks):
        if not isinstance(src_toks, list):
            src_toks = [src_toks] * 8
        sq_gate = None
        last = None
        for ec in range(8):
            sqtok = sS.tick()
            em.op("scalar", "activation", X2, src[:, ec, :], AF.Square,
                  wait=(src_toks[ec], sq_gate), inc=sqtok)
            for tq in range(4):
                w = [sqtok]
                if ec == 0:
                    w.append(ps_read[tq])
                mt = sT.tick() if tq == 3 else None
                em.op("tensor", "matmul", PA[0:1, tq * 512:(tq + 1) * 512],
                      C1B[:], X2[:, tq * 512:(tq + 1) * 512],
                      start=(ec == 0), stop=(ec == 7), wait=w, inc=mt)
                if tq == 3:
                    sq_gate = mt
            last = sq_gate
        return last

    def rs_row(tq, ss_tok, prev_b):
        qs = slice(tq * 512, (tq + 1) * 512)
        em.op("scalar", "activation", ROW3, PA[0:1, qs], AF.Ln,
              wait=(ss_tok,), inc=None)
        t = sS.tick()
        em.op("scalar", "activation", ROWB, ROW3, AF.Exp, scale=-0.5,
              wait=(prev_b,), inc=t)
        return t

    x_toks = []
    for dc in range(8):
        if dc % 2 == 0:
            tok = sQ.tick(16)
            em.op("sync", "dma_start", out=X[:, dc, :],
                  in_=xt[dc * 128:(dc + 1) * 128, :], wait=(), inc=tok)
        else:
            tok = sR.tick(16)
            em.op("sync", "dma_start", out=X[:, dc, :],
                  in_=xt[dc * 128:(dc + 1) * 128, :], wait=(), inc=tok)
        x_toks.append(tok)
    em.op("sync", "dma_start", out=LB[:], in_=lb_d[:], wait=(), inc=sQ.tick(16))
    em.op("sync", "dma_start", out=MUR[:], in_=mur_d[:], wait=(), inc=sQ.tick(16))
    t_small = sQ.tick(16)
    em.op("sync", "dma_start", out=MUK[:], in_=muk_d[:], wait=(), inc=t_small)
    em.op("gpsimd", "dma_start", out=VBR[:], in_=vbr_d[:], wait=(), inc=sG.tick(16))
    em.op("gpsimd", "dma_start", out=QBR[:], in_=qbr_d[:], wait=(), inc=sG.tick(16))
    t_vbr = sG.tick(16)
    em.op("gpsimd", "dma_start", out=KBR[:], in_=kbr_d[:], wait=(), inc=t_vbr)
    t_const = sV.tick()
    em.op("vector", "memset", CM1[:], 1.0 / 1024.0, wait=(), inc=None)
    em.op("vector", "memset", C1B[:], 1.0, wait=(), inc=None)
    em.op("vector", "memset", C1F[:], 1.0, wait=(), inc=None)
    em.op("vector", "memset", CC[:], 1.0 / 2048.0, wait=(), inc=None)
    em.op("vector", "memset", BN16[:], -16.0, wait=(), inc=None)
    em.op("vector", "memset", ONESR[:], 1.0, wait=(), inc=None)
    em.op("vector", "memset", EPSC[:], 1e-5, wait=(), inc=None)
    em.op("vector", "memset", ONESP[:], 0.0, wait=(), inc=None)
    em.op("vector", "memset", ONESP[0:1, :], 1.0, wait=(), inc=None)
    em.op("vector", "memset", EXPS[:, 1, :], 0.0, wait=(), inc=None)
    em.op("vector", "memset", VSTG[:, 0:4, :], 0.0, wait=(), inc=None)
    em.op("vector", "memset", EXPS[:, 2, :], 0.0, wait=(), inc=t_const)

    state = {"x_ready": x_toks}

    def attn_a():
        xn_q = layer_norm(state["x_ready"])

        k_c = [None] * 8

        def ev_k(ec, tq, pt, mmtok):
            tok = sS.tick()
            em.op("scalar", "activation",
                  B2[:, ec, tq * 512:(tq + 1) * 512], pt, AF.Copy,
                  wait=(mmtok,), inc=tok)
            k_c[ec] = tok
            return tok
        proj("aWk", XN, ev_k, rhs_q=xn_q)

        PHASE_MARKS.append(("attn_a:ssk", sT.n, sV.n, sS.n))
        ss_tok = sumsq_rows(B2, k_c)
        kn_tok = None
        kn_ec = [None] * 8
        prev_b = None
        for tq in range(4):
            qs = slice(tq * 512, (tq + 1) * 512)
            rs_s = (EXPS[:, 1, 1024 + (tq % 2) * 512:1536 + (tq % 2) * 512]
                    if tq < 2 else
                    EXPS[:, 2, (tq - 2) * 512:(tq - 1) * 512])
            t_rs = rs_row(tq, ss_tok, prev_b)
            t_b = sT.tick()
            em.op("tensor", "matmul", PB[:, qs], ONESP[:],
                  EXPS[:, 1, 512:1024],
                  start=True, stop=True, wait=(t_rs, ps_read[4 + tq]), inc=t_b)
            prev_b = t_b
            t_ev = sV.tick()
            em.op("vector", "tensor_copy", rs_s, PB[:, qs],
                  wait=(t_b, ln_bc_free[0]), inc=t_ev)
            for ec in range(8):
                tok = sV.tick()
                em.op("vector", "tensor_mul", B2[:, ec, qs], B2[:, ec, qs],
                      rs_s, wait=(t_ev,), inc=tok)
                kn_tok = tok
                if tq == 3:
                    kn_ec[ec] = tok
            ps_read[4 + tq] = t_ev
            ps_read[tq] = t_rs

        q_c = [None] * 8

        def ev_q(ec, tq, pt, mmtok):
            tok = sS.tick()
            em.op("scalar", "activation",
                  B1[:, ec, tq * 512:(tq + 1) * 512], pt, AF.Copy,
                  wait=(mmtok,), inc=tok)
            q_c[ec] = tok
            return tok
        proj("aWq", XN, ev_q)

        ss2 = sumsq_rows(B1, q_c)
        rq_s = []
        rq_rs = []
        for tq in range(4):
            qs = slice(tq * 512, (tq + 1) * 512)
            rs_s = (EXPS[:, 1, 1024 + (tq % 2) * 512:1536 + (tq % 2) * 512]
                    if tq < 2 else
                    EXPS[:, 2, (tq - 2) * 512:(tq - 1) * 512])
            rq_s.append(rs_s)
            em.op("scalar", "activation", ROW3, PA[0:1, qs], AF.Ln,
                  wait=(ss2,), inc=None)
            t = sS.tick()
            em.op("scalar", "activation", VSTG[0:1, tq, :], ROW3, AF.Exp,
                  scale=-0.5, wait=(), inc=t)
            rq_rs.append(t)
            ps_read[tq] = t

        fin_c = [None] * 8
        t_eq = [None] * 4
        kv_c = [None] * 8
        bcq = SCRB

        def pre_v(ec):
            if ec == 0:
                for tq in range(4):
                    qs = slice(tq * 512, (tq + 1) * 512)
                    t_b = sT.tick()
                    em.op("tensor", "matmul", PB[:, qs],
                          ONESP[:], VSTG[:, tq, :],
                          start=True, stop=True,
                          wait=(rq_rs[tq], ps_read[4 + tq]), inc=t_b)
                    t_ev = sV.tick()
                    em.op("vector", "tensor_copy", rq_s[tq], PB[:, qs],
                          wait=(t_b, kn_tok), inc=t_ev)
                    t_eq[tq] = t_ev
                    ps_read[4 + tq] = t_ev

        def ev_v(ec, tq, pt, mmtok):
            qs = slice(tq * 512, (tq + 1) * 512)
            tok = sV.tick()
            em.op("vector", "tensor_copy", X2[:, qs], pt,
                  wait=(mmtok,), inc=tok)
            if tq == 3:
                em.op("vector", "scalar_tensor_tensor", X2,
                      X2, CC[:], B2[:, ec, :],
                      ALU.mult, ALU.mult, accum_out=GR[:, ec:ec + 1],
                      wait=(kn_ec[ec],), inc=None)
                for tq2 in range(4):
                    qs2 = slice(tq2 * 512, (tq2 + 1) * 512)
                    em.op("vector", "scalar_tensor_tensor", bcq[:, 0:512],
                          B1[:, ec, qs2], GR[:, ec:ec + 1], rq_s[tq2],
                          ALU.mult, ALU.mult, wait=(t_eq[tq2],), inc=None)
                    tok = sV.tick()
                    em.op("vector", "tensor_add", X[:, ec, qs2],
                          X[:, ec, qs2], bcq[:, 0:512], wait=(), inc=tok)
                fin_c[ec] = tok
                kv_c[ec] = tok
            return tok
        proj("aWv", XN, ev_v, pre_ec=pre_v)
        ln_bc_free[0] = fin_c[7]
        state["x_ready"] = fin_c

    def ffn():
        def mix_hook(tq, tok):
            lo = tq * 512
            hi = (tq + 1) * 512
            s = 1 if tq == 0 else 0
            for dc in range(8):
                sh1 = XN[:, dc, lo + s:hi]
                sh0 = XN[:, dc, lo + s - 1:hi - 1]
                em.op("vector", "tensor_sub", X2[:, lo + s:hi], sh1, sh0,
                      wait=(tok,), inc=None)
                em.op("vector", "scalar_tensor_tensor", B1[:, dc, lo + s:hi],
                      X2[:, lo + s:hi], MUR[:, dc:dc + 1], sh0,
                      ALU.mult, ALU.add, wait=(), inc=None)
                t2 = sV.tick()
                em.op("vector", "scalar_tensor_tensor", B2[:, dc, lo + s:hi],
                      X2[:, lo + s:hi], MUK[:, dc:dc + 1], sh0,
                      ALU.mult, ALU.add, wait=(), inc=t2)
                if tq == 0:
                    em.op("vector", "tensor_scalar", B1[:, dc, 0:1],
                          XN[:, dc, 0:1], MUR[:, dc:dc + 1], None, ALU.mult,
                          wait=(), inc=None)
                    t2 = sV.tick()
                    em.op("vector", "tensor_scalar", B2[:, dc, 0:1],
                          XN[:, dc, 0:1], MUK[:, dc:dc + 1], None, ALU.mult,
                          wait=(), inc=t2)
            return t2

        mix_q = layer_norm(state["x_ready"], qhook=mix_hook)

        def ev_r(ec, tq, pt, mmtok):
            tok = sS.tick()
            em.op("scalar", "activation",
                  XN[:, ec, tq * 512:(tq + 1) * 512], pt, AF.Sigmoid,
                  wait=(mmtok,), inc=tok)
            return tok
        PHASE_MARKS.append(("ffn:rp", sT.n, sV.n, sS.n))
        r_last_mm = proj("fWr", B1, ev_r, rhs_q=mix_q)

        prev_sq = [None]
        kk_q = [None] * 4

        def ev_kk(ec, tq, pt, mmtok):
            qs = slice(tq * 512, (tq + 1) * 512)
            t1 = sS.tick()
            em.op("scalar", "activation", X2[:, qs], pt, AF.Relu,
                  wait=(mmtok, prev_sq[0]), inc=t1)
            t2 = sV.tick()
            em.op("vector", "tensor_mul", B1[:, ec, qs], X2[:, qs], X2[:, qs],
                  wait=(t1, r_last_mm), inc=t2)
            prev_sq[0] = t2
            if ec == 7:
                kk_q[tq] = t2
            return t2
        proj("fWk", B2, ev_kk, rhs_q=mix_q)
        PHASE_MARKS.append(("ffn:vp", sT.n, sV.n, sS.n))

        fin_c = [None] * 8

        def ev_vv(ec, tq, pt, mmtok):
            qs = slice(tq * 512, (tq + 1) * 512)
            em.op("vector", "scalar_tensor_tensor", SCR[:], pt,
                  C1F[:], XN[:, ec, qs], ALU.mult, ALU.mult,
                  wait=(mmtok,), inc=None)
            tok = sV.tick()
            em.op("vector", "tensor_add", X[:, ec, qs], X[:, ec, qs], SCR[:],
                  wait=(), inc=tok)
            fin_c[ec] = tok
            return tok
        proj("fWv", B1, ev_vv, rhs_q=kk_q)
        state["x_ready"] = fin_c

    def attn_b():
        xn_q = layer_norm(state["x_ready"])
        xn_tok = xn_q[3]

        PHASE_MARKS.append(("attn_b:vproj", sT.n, sV.n, sS.n))
        wv = wdr["bVw"]
        vgate = sT.now()
        mm_toks = {}
        vst_free = [None] * 4
        vd_tok = None
        nstg = 0
        for grp in range(2):
            for hpq in range(2):
                hs = slice(hpq * 512, (hpq + 1) * 512)
                pts = list(range(8))
                for dc in range(8):
                    s = dc % 4
                    g = grp * 2 + hpq
                    lw = [mm_toks.get((g, dc - 4)) if dc >= 4 else vgate]
                    ltok = sG.tick(16)
                    em.op("gpsimd", "dma_start",
                          out=WBUFf[:, s * 512:(s + 1) * 512],
                          in_=wv[dc * 128:(dc + 1) * 128, hs],
                          wait=lw, inc=ltok)
                    for i, sc in enumerate(range(grp * 8, grp * 8 + 8)):
                        pi = pts[i]
                        w = [ltok]
                        if dc == 0:
                            w.append(ps_read[pi])
                            if hpq == 0 and i == 0:
                                w.append(xn_q[grp * 2 + (i >= 4)])
                        tok = sT.tick() if i == 7 else None
                        em.op("tensor", "matmul", pslot_ap(pi),
                              XN[:, dc, sc * 128:(sc + 1) * 128],
                              WBUFf[:, s * 512:(s + 1) * 512],
                              start=(dc == 0), stop=False,
                              wait=w, inc=tok)
                    mm_toks[(g, dc)] = tok
                vgate = mm_toks[(g, 7)]
                for i, sc in enumerate(range(grp * 8, grp * 8 + 8)):
                    pi = pts[i]
                    mt = sT.tick()
                    em.op("tensor", "matmul", pslot_ap(pi), ONESR[0:1, 0:128],
                          VBR[0:1, hs], start=False, stop=True,
                          wait=(t_vbr, t_const), inc=mt)
                    vgate = mt
                    par = nstg % 4
                    nstg += 1
                    vt = sV.tick()
                    em.op("vector", "tensor_copy", VSTG[:, 6 + par, :],
                          pslot_ap(pi), wait=(mt, vst_free[par]), inc=vt)
                    ps_read[pi] = vt
                    dt = sR.tick(16)
                    em.op("sync", "dma_start",
                          out=vdram[sc * 128:(sc + 1) * 128, hs],
                          in_=VSTG[:, 6 + par, :], wait=(vt,), inc=dt)
                    vst_free[par] = dt
                    vd_tok = dt
                wslot_free[0] = vgate
        wslot_free[1] = vgate
        v_done = vd_tok

        PHASE_MARKS.append(("attn_b:loop", sT.n, sV.n, sS.n))
        def ev_bq(ec, tq, pt, mmtok):
            tok = sS.tick()
            em.op("scalar", "activation",
                  B1[:, ec, tq * 512:(tq + 1) * 512], pt, AF.Copy,
                  wait=(mmtok,), inc=tok)
            return tok
        proj("bQw", XN, ev_bq, rhs_q=xn_q,
             stationary_extra=(
                 lambda ec, tq: QBR[0:1, ec * 128:(ec + 1) * 128],
                 lambda ec, tq: ONESR[0:1, 0:512], t_vbr))

        def ev_bk(ec, tq, pt, mmtok):
            tok = sS.tick()
            em.op("scalar", "activation",
                  B2[:, ec, tq * 512:(tq + 1) * 512], pt, AF.Copy,
                  wait=(mmtok,), inc=tok)
            return tok
        proj("bKw", XN, ev_bk, rhs_q=xn_q,
             stationary_extra=(
                 lambda ec, tq: KBR[0:1, ec * 128:(ec + 1) * 128],
                 lambda ec, tq: ONESR[0:1, 0:512], t_vbr))

        first_w = [t for t in ps_read[0:4] if t is not None]
        pb_free = [t for t in ps_read[4:8] if t is not None]
        kb_done = sS.now()
        exp_tok = {}
        sc_tok = {}
        ctx_tok = {}
        vsc_tok = {}
        z_tok = {}
        sc_last = {}
        ktp_tok = {}
        evac = {"tok": None}
        NIT = NH * NKT

        def prep_ktp(h):
            half_p = (h % 2) * 64
            s = h % 2
            w = [kb_done, sc_last.get(h - 2)]
            em.op("vector", "memset", KTP[64 - half_p:128 - half_p, s, :], 0.0,
                  wait=w, inc=None)
            tok = sV.tick()
            em.op("vector", "tensor_copy", KTP[half_p:half_p + 64, s, :],
                  B2[half_p:half_p + 64, h // 2, :], wait=(), inc=tok)
            ktp_tok[h] = tok

        def emit_scores(it, qh):
            par3 = it % 3
            h = it // NKT
            kt = it % NKT
            chunk = h // 2
            if qh == 0:
                if kt == 0:
                    if h == 0:
                        prep_ktp(0)
                    if h + 1 < NH:
                        prep_ktp(h + 1)
                vr = sR.tick(16)
                em.op("sync", "dma_start", out=VRING[:, it % 6, :],
                      in_=vdram[kt * 128:(kt + 1) * 128,
                                h * HP:(h + 1) * HP],
                      wait=(v_done, vsc_tok.get(it - 6)), inc=vr)
                sc_state[it] = {"vr": vr}
            for qq in (2 * qh, 2 * qh + 1):
                w = [exp_tok.get((it - 1, qh))]
                if qq == 2 * qh:
                    w.append(ktp_tok[h])
                    if it == 0:
                        w += first_w
                tok = sT.tick() if qq == 2 * qh + 1 else None
                em.op("tensor", "matmul", PA[:, qq * 512:(qq + 1) * 512],
                      KTP[:, h % 2, kt * 128:(kt + 1) * 128],
                      B1[:, chunk, qq * 512:(qq + 1) * 512],
                      start=True, stop=True, wait=w, inc=tok)
            sc_tok[(it, qh)] = tok
            hs = slice(qh * 1024, (qh + 1) * 1024)
            par = it % 4
            et = sS.tick()
            em.op("scalar", "activation",
                  EXPS[:, par3, hs], PA[:, hs],
                  AF.Exp, bias=BN16[:], scale=0.125,
                  accum_out=ZH[:, par, qh:qh + 1],
                  wait=(tok, t_const, ctx_tok.get(it - 3)), inc=et)
            exp_tok[(it, qh)] = et
            if qh == 1:
                sc_last[h] = tok

        def emit_scores_tail(it):
            par = it % 4
            par3 = it % 3
            h = it // NKT
            kt = it % NKT
            half_p = (h % 2) * 64
            vr = sc_state.pop(it)["vr"]
            t_za = sV.tick()
            em.op("vector", "tensor_add", ZS[:, par:par + 1],
                  ZH[:, par, 0:1], ZH[:, par, 1:2],
                  wait=(exp_tok[(it, 0)], exp_tok[(it, 1)]), inc=t_za)
            t_rc = sV.tick()
            em.op("vector", "reciprocal", RZ[:, par:par + 1],
                  ZS[:, par:par + 1], wait=(t_za,), inc=t_rc)
            if kt < 6:
                em.op("vector", "memset",
                      VSC[:, it % 6, 64 - half_p:128 - half_p],
                      0.0, wait=(ctx_tok.get(it - 6),), inc=None)
            vt = sV.tick()
            em.op("vector", "tensor_scalar",
                  VSC[:, it % 6, half_p:half_p + 64],
                  VRING[:, it % 6, :], RZ[:, par:par + 1], None, ALU.mult,
                  wait=(vr, t_rc, ctx_tok.get(it - 6)), inc=vt)
            vsc_tok[it] = vt

        sc_state = {}
        def emit_ctx_half(it, half):
            par = it % 4
            par3 = it % 3
            h = it // NKT
            kt = it % NKT
            chunk = h // 2
            tok = None
            for qq in (2 * half, 2 * half + 1):
                w = [vsc_tok[it]]
                if kt == 0 and qq == 0:
                    w.append(evac["tok"])
                    if it <= 1:
                        w += pb_free
                tok = sT.tick() if qq == 3 else None
                em.op("tensor", "matmul",
                      PB[:, qq * 512:(qq + 1) * 512],
                      VSC[:, it % 6, :],
                      EXPS[:, par3, qq * 512:(qq + 1) * 512],
                      start=(kt == 0 and h % 2 == 0),
                      stop=(kt == NKT - 1 and h % 2 == 1),
                      wait=w, inc=tok)
            if qq == 3:
                ctx_tok[it] = tok
                if kt == NKT - 1 and h % 2 == 1:
                    t = sV.tick()
                    em.op("vector", "tensor_copy", XN[:, chunk, :],
                          PB[:, 0:T], wait=(tok,), inc=t)
                    evac["tok"] = t

        for it in range(NIT + 2):
            if it < NIT:
                emit_scores(it, 0)
                emit_scores(it, 1)
                emit_scores_tail(it)
            if it >= 2:
                emit_ctx_half(it - 2, 0)
                emit_ctx_half(it - 2, 1)
        evac_tok = evac["tok"]
        for i in range(8):
            ps_read[i] = exp_tok[(NIT - 1, i // 2)] if i < 4 else evac_tok

        PHASE_MARKS.append(("attn_b:outproj", sT.n, sV.n, sS.n))
        fin_c = [None] * 8

        def ev_o(ec, tq, pt, mmtok):
            qs = slice(tq * 512, (tq + 1) * 512)
            tok = sV.tick()
            em.op("vector", "scalar_tensor_tensor", X[:, ec, qs], pt,
                  LB[:, ec:ec + 1], X[:, ec, qs], ALU.add, ALU.add,
                  wait=(mmtok, t_small), inc=tok)
            fin_c[ec] = tok
            return tok
        proj("bLw", XN, ev_o, rhs_ready=evac_tok)
        state["x_ready"] = fin_c

    subs = [attn_a, ffn, attn_b, ffn]
    names = ["attn_a", "ffn1", "attn_b", "ffn2"]
    PHASE_MARKS.clear()
    for nm, fsub in zip(names[:N_SUB], subs[:N_SUB]):
        PHASE_MARKS.append((nm + ":start", sT.n, sV.n, sS.n))
        fsub()
    PHASE_MARKS.append(("end", sT.n, sV.n, sS.n))

    xr_fin = state["x_ready"]
    if not isinstance(xr_fin, list):
        xr_fin = [xr_fin] * 8
    out_toks = []
    for dc in range(8):
        tok = sQ.tick(16)
        em.op("sync", "dma_start", out=y[dc * 128:(dc + 1) * 128, :],
              in_=X[:, dc, :], wait=(xr_fin[dc],), inc=tok)
        out_toks.append(tok)
    if DBG:
        dXN = nc.dram_tensor("dXN", [D, T], BF16, kind="ExternalOutput")
        dB1 = nc.dram_tensor("dB1", [D, T], BF16, kind="ExternalOutput")
        dB2 = nc.dram_tensor("dB2", [D, T], BF16, kind="ExternalOutput")
        dGR = nc.dram_tensor("dGR", [128, 8], F32, kind="ExternalOutput")
        for dc in range(8):
            em.op("sync", "dma_start", out=dXN[dc * 128:(dc + 1) * 128, :],
                  in_=XN[:, dc, :], wait=(state["x_ready"],), inc=sQ.tick(16))
            em.op("sync", "dma_start", out=dB1[dc * 128:(dc + 1) * 128, :],
                  in_=B1[:, dc, :], wait=(), inc=sQ.tick(16))
            em.op("sync", "dma_start", out=dB2[dc * 128:(dc + 1) * 128, :],
                  in_=B2[:, dc, :], wait=(), inc=sQ.tick(16))
        out_toks.append(sQ.tick(16))
        em.op("sync", "dma_start", out=dGR[:], in_=GR[:], wait=(),
              inc=out_toks[-1])
    em.wait("sync", out_toks[-1])

    with nc.Block() as block:
        @block.sync
        def _(e):
            seen = {}
            for f in em.q["sync"]:
                f(e, seen)

        @block.gpsimd
        def _(e):
            seen = {}
            for f in em.q["gpsimd"]:
                f(e, seen)

        @block.tensor
        def _(e):
            seen = {}
            for f in em.q["tensor"]:
                f(e, seen)

        @block.vector
        def _(e):
            seen = {}
            for f in em.q["vector"]:
                f(e, seen)

        @block.scalar
        def _(e):
            seen = {}
            for f in em.q["scalar"]:
                f(e, seen)

    stack.close()
    return nc


_NC_CACHE = {}


def _get_nc():
    if "nc" not in _NC_CACHE:
        _NC_CACHE["nc"] = build_nc()
    return _NC_CACHE["nc"]


def kernel(**inputs):
    f = lambda k: np.ascontiguousarray(np.asarray(inputs[k], dtype=np.float32))
    x = f("L0_attn_a_x")
    tr = lambda a: np.ascontiguousarray(a.T)

    wm = {
        "aWq": tr(f("L0_attn_a_W_q")),
        "aWk": tr(f("L0_attn_a_W_k")),
        "aWv": tr(f("L0_attn_a_W_v")),
        "fWr": tr(f("L0_ffn_W_r")),
        "fWk": tr(f("L0_ffn_W_k")),
        "fWv": tr(f("L0_ffn_W_v")),
        "bQw": tr(f("L0_attn_b_q_w").reshape(D, D)),
        "bKw": tr(f("L0_attn_b_k_w").reshape(D, D)),
        "bVw": tr(f("L0_attn_b_v_w").reshape(D, D)),
        "bLw": tr(f("L0_attn_b_l_w")),
        "qbr": f("L0_attn_b_q_b").reshape(1, D),
        "kbr": f("L0_attn_b_k_b").reshape(1, D),
        "lb": np.ascontiguousarray(f("L0_attn_b_l_b").reshape(8, 128).T),
        "mur": np.ascontiguousarray(f("L0_ffn_mu_r").reshape(8, 128).T),
        "muk": np.ascontiguousarray(f("L0_ffn_mu_k").reshape(8, 128).T),
        "vbr": f("L0_attn_b_v_b").reshape(1, D),
    }

    nc = _get_nc()
    in_maps = []
    for b in range(8):
        m = dict(wm)
        m["xt"] = np.ascontiguousarray(x[b].T)
        in_maps.append(m)
    res = run_bass_kernel_spmd(nc, in_maps, core_ids=list(range(8)))
    out = np.stack([np.asarray(res.results[b]["y"]).T for b in range(8)])
    return np.ascontiguousarray(out.astype(np.float32))


# version 41
# speedup vs baseline: 1.0237x; 1.0104x over previous
import contextlib
import numpy as np
import concourse.bass as bass
import concourse.mybir as mybir
from concourse.bass_utils import run_bass_kernel_spmd

F32 = mybir.dt.float32
BF16 = mybir.dt.bfloat16
AF = mybir.ActivationFunctionType
ALU = mybir.AluOpType
AX = mybir.AxisListType

D = 1024
T = 2048
NH = 16
HP = 64
NKT = 16

N_SUB = 4
FP64_SAFE = False
DBG = False

ENGS = ("sync", "gpsimd", "tensor", "vector", "scalar")


class Sem:
    def __init__(self, h):
        self.h = h
        self.n = 0

    def tick(self, k=1):
        self.n += k
        return (self, self.n, k)

    def now(self):
        return (self, self.n, 0)


class Em:

    def __init__(self):
        self.q = {e: [] for e in ENGS}
        self.last_dma = {}

    def op(self, eng, meth, *args, wait=(), inc=None, **kw):
        if FP64_SAFE and meth == "dma_start" and inc is not None:
            prev = self.last_dma.get(id(inc[0]))
            if prev is not None:
                wait = tuple(wait) + (prev,)
            self.last_dma[id(inc[0])] = inc
        waits = tuple((t[0], t[1]) for t in wait if t is not None and t[1] > 0)

        def f(e, seen, meth=meth, args=args, kw=kw, waits=waits, inc=inc):
            for (s, v) in waits:
                if seen.get(id(s), 0) < v:
                    e.wait_ge(s.h, v)
                    seen[id(s)] = v
            ins = getattr(e, meth)(*args, **kw)
            if inc is not None:
                ins.then_inc(inc[0].h, inc[2])
        self.q[eng].append(f)

    def wait(self, eng, tok):
        if tok is None or tok[1] <= 0:
            return

        def f(e, seen, s=tok[0], v=tok[1]):
            if seen.get(id(s), 0) < v:
                e.wait_ge(s.h, v)
                seen[id(s)] = v
        self.q[eng].append(f)


PHASE_MARKS = []


def build_nc():
    nc = bass.Bass("TRN2", target_bir_lowering=False,
                   dynamic_dma_scratch_size=16384)

    xt = nc.dram_tensor("xt", [D, T], F32, kind="ExternalInput")
    y = nc.dram_tensor("y", [D, T], F32, kind="ExternalOutput")
    wdr = {}
    for nm in ("aWq", "aWk", "aWv", "fWr", "fWk", "fWv", "bQw", "bKw", "bVw", "bLw"):
        wdr[nm] = nc.dram_tensor(nm, [D, D], F32, kind="ExternalInput")
    lb_d = nc.dram_tensor("lb", [128, 8], F32, kind="ExternalInput")
    mur_d = nc.dram_tensor("mur", [128, 8], F32, kind="ExternalInput")
    muk_d = nc.dram_tensor("muk", [128, 8], F32, kind="ExternalInput")
    vbr_d = nc.dram_tensor("vbr", [1, D], F32, kind="ExternalInput")
    qbr_d = nc.dram_tensor("qbr", [1, D], F32, kind="ExternalInput")
    kbr_d = nc.dram_tensor("kbr", [1, D], F32, kind="ExternalInput")
    vdram = nc.dram_tensor("vdram", [T, D], BF16, kind="Internal")

    X = nc.alloc_sbuf_tensor("X", [128, 8, T], F32)
    XN = nc.alloc_sbuf_tensor("XN", [128, 8, T], BF16)
    B1 = nc.alloc_sbuf_tensor("B1", [128, 8, T], BF16)
    B2 = nc.alloc_sbuf_tensor("B2", [128, 8, T], BF16)
    WBUF = nc.alloc_sbuf_tensor("WBUF", [128, 3, 8, 128], BF16)
    KTP = nc.alloc_sbuf_tensor("KTP", [128, 2, 2048], BF16)
    EXPS = nc.alloc_sbuf_tensor("EXPS", [128, 3, T], BF16)
    VSTG = nc.alloc_sbuf_tensor("VSTG", [128, 10, 512], BF16)
    VRING = nc.alloc_sbuf_tensor("VRING", [128, 6, HP], BF16)
    VSC = nc.alloc_sbuf_tensor("VSC", [128, 6, 128], BF16)
    SCR = nc.alloc_sbuf_tensor("SCR", [128, 512], F32)
    ONESR = nc.alloc_sbuf_tensor("ONESR", [1, 512], BF16)
    LB = nc.alloc_sbuf_tensor("LB", [128, 8], F32)
    MUR = nc.alloc_sbuf_tensor("MUR", [128, 8], F32)
    MUK = nc.alloc_sbuf_tensor("MUK", [128, 8], F32)
    GR = nc.alloc_sbuf_tensor("GR", [128, 8], F32)
    ZH = nc.alloc_sbuf_tensor("ZH", [128, 4, 4], F32)
    ZS = nc.alloc_sbuf_tensor("ZS", [128, 4], F32)
    RZ = nc.alloc_sbuf_tensor("RZ", [128, 4], F32)
    CM1 = nc.alloc_sbuf_tensor("CM1", [128, 1], BF16)
    C1B = nc.alloc_sbuf_tensor("C1B", [128, 1], BF16)
    C1F = nc.alloc_sbuf_tensor("C1F", [128, 1], F32)
    CC = nc.alloc_sbuf_tensor("CC", [128, 1], F32)
    BN16 = nc.alloc_sbuf_tensor("BN16", [128, 1], F32)
    EPSC = nc.alloc_sbuf_tensor("EPSC", [1, 8], F32)
    ONESP = nc.alloc_sbuf_tensor("ONESP", [128, 128], BF16)

    X2 = EXPS[:, 0, :]
    ROWAf = EXPS[0:1, 0, :]
    ROWBf = EXPS[0:1, 2, :]
    ROWA = EXPS[0:1, 1, 0:512]
    ROWB = EXPS[0:1, 1, 512:1024]
    VSTGf = VSTG.bitcast(F32).reshape([128, 2560])
    SCRB = SCR.bitcast(BF16).reshape([128, 1024])
    JNK = SCRB
    ROW2f = VSTGf[0:1, 0:2048]
    ROW3 = VSTGf[0:1, 2048:2560]
    WBUFf = WBUF.reshape([128, 3072])
    VBR = KTP[0:1, 0, 0:1024]
    QBR = KTP[0:1, 0, 1024:2048]
    KBR = KTP[0:1, 1, 0:1024]

    PA = nc.alloc_psum_tensor("PA", [128, T], F32)
    PB = nc.alloc_psum_tensor("PB", [128, T], F32)

    em = Em()
    stack = contextlib.ExitStack()
    sQ = Sem(stack.enter_context(nc.semaphore("sQ")))
    sG = Sem(stack.enter_context(nc.semaphore("sG")))
    sT = Sem(stack.enter_context(nc.semaphore("sT")))
    sV = Sem(stack.enter_context(nc.semaphore("sV")))
    sS = Sem(stack.enter_context(nc.semaphore("sS")))
    sR = Sem(stack.enter_context(nc.semaphore("sR")))

    def pslot_ap(i):
        base = PA if i < 4 else PB
        q = i % 4
        return base[:, q * 512:(q + 1) * 512]
    ps_read = [None] * 8

    wslot_free = [None, None, None]
    wring = [0]

    def proj(wname, rhs_tensor, evac, rhs_ready=None, rhs_q=None,
             stationary_extra=None, extra_wait=None, pre_ec=None,
             rhs_dq=None):
        wd = wdr[wname]
        wap = wd.rearrange("(dc p) (ec j) -> p dc ec j", p=128, j=128)
        ec_last_tok = {}
        last_tok = None
        for ec in range(8):
            if pre_ec is not None:
                pre_ec(ec)
            s = wring[0] % 3
            wring[0] += 1
            ltw = [wslot_free[s]]
            if extra_wait is not None and ec == 0:
                ltw.append(extra_wait)
            ltok = sG.tick(16)
            em.op("gpsimd", "dma_start",
                  out=WBUF[:, s, :, :], in_=wap[:, :, ec, :],
                  wait=ltw, inc=ltok)
            for tq in range(4):
                pi = (ec * 4 + tq) % 8
                pt = pslot_ap(pi)
                for dc in range(8):
                    w = []
                    if dc == 0:
                        w = [ps_read[pi], ltok]
                        if rhs_q is not None and ec > 0:
                            w.append(rhs_q[tq])
                        if ec == 0 and tq == 0 and rhs_ready is not None:
                            w.append(rhs_ready)
                    if ec == 0 and rhs_dq is not None:
                        w.append(rhs_dq.get((dc, tq)))
                    is_last = (dc == 7 and stationary_extra is None)
                    tok = sT.tick() if is_last else None
                    em.op("tensor", "matmul", pt,
                          WBUF[:, s, dc, :],
                          rhs_tensor[:, dc, tq * 512:(tq + 1) * 512],
                          start=(dc == 0), stop=is_last,
                          wait=w, inc=tok)
                if stationary_extra is not None:
                    lhsT_fn, rhs_fn, bias_tok = stationary_extra
                    tok = sT.tick()
                    em.op("tensor", "matmul", pt, lhsT_fn(ec, tq), rhs_fn(ec, tq),
                          start=False, stop=True, wait=(bias_tok, t_const), inc=tok)
                ps_read[pi] = evac(ec, tq, pt, tok)
                last_tok = tok
            wslot_free[s] = last_tok
            ec_last_tok[ec] = last_tok
        return last_tok

    ln_bc_free = [None]

    def layer_norm(x_ready, qhook=None):
        if not isinstance(x_ready, list):
            x_ready = [x_ready] * 8
        cast_toks = []
        for dc in range(8):
            tok = sV.tick()
            em.op("vector", "tensor_copy", XN[:, dc, :], X[:, dc, :],
                  wait=(x_ready[dc],), inc=tok)
            cast_toks.append(tok)
        sq_gate = None
        stat_tok = None
        for dc in range(8):
            sqtok = sS.tick()
            em.op("scalar", "activation", X2, XN[:, dc, :], AF.Square,
                  wait=(cast_toks[dc], sq_gate), inc=sqtok)
            for tq in range(4):
                w_m = [cast_toks[dc]]
                w_s = [sqtok]
                if dc == 0:
                    w_m.append(ps_read[tq])
                    w_s.append(ps_read[4 + tq])
                mt = sT.tick() if tq == 3 else None
                em.op("tensor", "matmul", PA[0:1, tq * 512:(tq + 1) * 512],
                      CM1[:], XN[:, dc, tq * 512:(tq + 1) * 512],
                      start=(dc == 0), stop=(dc == 7), wait=w_m, inc=None)
                em.op("tensor", "matmul", PB[0:1, tq * 512:(tq + 1) * 512],
                      CM1[:], X2[:, tq * 512:(tq + 1) * 512],
                      start=(dc == 0), stop=(dc == 7), wait=w_s, inc=mt)
                if tq == 3:
                    sq_gate = mt
            stat_tok = sq_gate
        xn_tok = None
        xn_q = []
        xn_dq = {}
        t_mb = sS.tick()
        em.op("scalar", "activation", ROWAf, PA[0:1, 0:T], AF.Copy,
              wait=(stat_tok,), inc=t_mb)
        t_msq = sS.tick()
        em.op("scalar", "activation", ROW2f, PA[0:1, 0:T], AF.Square,
              wait=(), inc=t_msq)
        t_var = sV.tick()
        em.op("vector", "scalar_tensor_tensor", ROW2f, PB[0:1, 0:T],
              EPSC[0:1, 0:1], ROW2f, ALU.add, ALU.subtract,
              wait=(t_msq, t_const), inc=t_var)
        t_exs = []
        for tq in range(4):
            em.op("scalar", "activation", ROW3,
                  ROW2f[0:1, tq * 512:(tq + 1) * 512], AF.Ln,
                  wait=(t_var,) if tq == 0 else (), inc=None)
            t_ex = sS.tick()
            em.op("scalar", "activation",
                  ROWBf[0:1, tq * 512:(tq + 1) * 512], ROW3, AF.Exp,
                  scale=-0.5, wait=(), inc=t_ex)
            t_exs.append(t_ex)
        for tq in range(4):
            qs = slice(tq * 512, (tq + 1) * 512)
            t_bm = sT.tick()
            em.op("tensor", "matmul", PA[:, qs], ONESP[:],
                  X2[:, qs],
                  start=True, stop=True, wait=(t_msq,), inc=t_bm)
            t_br = sT.tick()
            em.op("tensor", "matmul", PB[:, qs], ONESP[:],
                  EXPS[:, 2, qs],
                  start=True, stop=True, wait=(t_exs[tq],), inc=t_br)
            mb_s = EXPS[:, 1, 1024:1536]
            rb_s = EXPS[:, 1, 1536:2048]
            t_eb = sV.tick()
            em.op("vector", "tensor_copy", mb_s, PA[:, qs],
                  wait=(t_bm, ln_bc_free[0]), inc=None)
            em.op("vector", "tensor_copy", rb_s, PB[:, qs],
                  wait=(t_br,), inc=t_eb)
            for dc in range(8):
                em.op("vector", "tensor_sub", XN[:, dc, qs], XN[:, dc, qs],
                      mb_s, wait=(t_eb,), inc=None)
                tok = sV.tick()
                em.op("vector", "tensor_mul", XN[:, dc, qs], XN[:, dc, qs],
                      rb_s, wait=(), inc=tok)
                xn_tok = tok
                xn_dq[(dc, tq)] = tok
            ln_bc_free[0] = xn_tok
            ps_read[tq] = t_eb
            ps_read[4 + tq] = t_eb
            if qhook is not None:
                xn_tok = qhook(tq, xn_tok)
            xn_q.append(xn_tok)
        return xn_q, xn_dq

    def sumsq_rows(src, src_toks):
        if not isinstance(src_toks, list):
            src_toks = [src_toks] * 8
        sq_gate = None
        last = None
        for ec in range(8):
            sqtok = sS.tick()
            em.op("scalar", "activation", X2, src[:, ec, :], AF.Square,
                  wait=(src_toks[ec], sq_gate), inc=sqtok)
            for tq in range(4):
                w = [sqtok]
                if ec == 0:
                    w.append(ps_read[tq])
                mt = sT.tick() if tq == 3 else None
                em.op("tensor", "matmul", PA[0:1, tq * 512:(tq + 1) * 512],
                      C1B[:], X2[:, tq * 512:(tq + 1) * 512],
                      start=(ec == 0), stop=(ec == 7), wait=w, inc=mt)
                if tq == 3:
                    sq_gate = mt
            last = sq_gate
        return last

    def rs_row(tq, ss_tok, prev_b):
        qs = slice(tq * 512, (tq + 1) * 512)
        em.op("scalar", "activation", ROW3, PA[0:1, qs], AF.Ln,
              wait=(ss_tok,), inc=None)
        t = sS.tick()
        em.op("scalar", "activation", ROWB, ROW3, AF.Exp, scale=-0.5,
              wait=(prev_b,), inc=t)
        return t

    x_toks = []
    for dc in range(8):
        if dc % 2 == 0:
            tok = sQ.tick(16)
            em.op("sync", "dma_start", out=X[:, dc, :],
                  in_=xt[dc * 128:(dc + 1) * 128, :], wait=(), inc=tok)
        else:
            tok = sR.tick(16)
            em.op("sync", "dma_start", out=X[:, dc, :],
                  in_=xt[dc * 128:(dc + 1) * 128, :], wait=(), inc=tok)
        x_toks.append(tok)
    em.op("sync", "dma_start", out=LB[:], in_=lb_d[:], wait=(), inc=sQ.tick(16))
    em.op("sync", "dma_start", out=MUR[:], in_=mur_d[:], wait=(), inc=sQ.tick(16))
    t_small = sQ.tick(16)
    em.op("sync", "dma_start", out=MUK[:], in_=muk_d[:], wait=(), inc=t_small)
    em.op("gpsimd", "dma_start", out=VBR[:], in_=vbr_d[:], wait=(), inc=sG.tick(16))
    em.op("gpsimd", "dma_start", out=QBR[:], in_=qbr_d[:], wait=(), inc=sG.tick(16))
    t_vbr = sG.tick(16)
    em.op("gpsimd", "dma_start", out=KBR[:], in_=kbr_d[:], wait=(), inc=t_vbr)
    t_const = sV.tick()
    em.op("vector", "memset", CM1[:], 1.0 / 1024.0, wait=(), inc=None)
    em.op("vector", "memset", C1B[:], 1.0, wait=(), inc=None)
    em.op("vector", "memset", C1F[:], 1.0, wait=(), inc=None)
    em.op("vector", "memset", CC[:], 1.0 / 2048.0, wait=(), inc=None)
    em.op("vector", "memset", BN16[:], -16.0, wait=(), inc=None)
    em.op("vector", "memset", ONESR[:], 1.0, wait=(), inc=None)
    em.op("vector", "memset", EPSC[:], 1e-5, wait=(), inc=None)
    em.op("vector", "memset", ONESP[:], 0.0, wait=(), inc=None)
    em.op("vector", "memset", ONESP[0:1, :], 1.0, wait=(), inc=None)
    em.op("vector", "memset", EXPS[:, 1, :], 0.0, wait=(), inc=None)
    em.op("vector", "memset", VSTG[:, 0:4, :], 0.0, wait=(), inc=None)
    em.op("vector", "memset", EXPS[:, 2, :], 0.0, wait=(), inc=t_const)

    state = {"x_ready": x_toks}

    def attn_a():
        xn_q, xn_dq = layer_norm(state["x_ready"])

        k_c = [None] * 8

        def ev_k(ec, tq, pt, mmtok):
            tok = sS.tick()
            em.op("scalar", "activation",
                  B2[:, ec, tq * 512:(tq + 1) * 512], pt, AF.Copy,
                  wait=(mmtok,), inc=tok)
            k_c[ec] = tok
            return tok
        proj("aWk", XN, ev_k, rhs_q=xn_q, rhs_dq=xn_dq)

        PHASE_MARKS.append(("attn_a:ssk", sT.n, sV.n, sS.n))
        ss_tok = sumsq_rows(B2, k_c)
        kn_tok = None
        kn_ec = [None] * 8
        prev_b = None
        for tq in range(4):
            qs = slice(tq * 512, (tq + 1) * 512)
            rs_s = (EXPS[:, 1, 1024 + (tq % 2) * 512:1536 + (tq % 2) * 512]
                    if tq < 2 else
                    EXPS[:, 2, (tq - 2) * 512:(tq - 1) * 512])
            t_rs = rs_row(tq, ss_tok, prev_b)
            t_b = sT.tick()
            em.op("tensor", "matmul", PB[:, qs], ONESP[:],
                  EXPS[:, 1, 512:1024],
                  start=True, stop=True, wait=(t_rs, ps_read[4 + tq]), inc=t_b)
            prev_b = t_b
            t_ev = sV.tick()
            em.op("vector", "tensor_copy", rs_s, PB[:, qs],
                  wait=(t_b, ln_bc_free[0]), inc=t_ev)
            for ec in range(8):
                tok = sV.tick()
                em.op("vector", "tensor_mul", B2[:, ec, qs], B2[:, ec, qs],
                      rs_s, wait=(t_ev,), inc=tok)
                kn_tok = tok
                if tq == 3:
                    kn_ec[ec] = tok
            ps_read[4 + tq] = t_ev
            ps_read[tq] = t_rs

        q_c = [None] * 8

        def ev_q(ec, tq, pt, mmtok):
            tok = sS.tick()
            em.op("scalar", "activation",
                  B1[:, ec, tq * 512:(tq + 1) * 512], pt, AF.Copy,
                  wait=(mmtok,), inc=tok)
            q_c[ec] = tok
            return tok
        proj("aWq", XN, ev_q)

        ss2 = sumsq_rows(B1, q_c)
        rq_s = []
        rq_rs = []
        for tq in range(4):
            qs = slice(tq * 512, (tq + 1) * 512)
            rs_s = (EXPS[:, 1, 1024 + (tq % 2) * 512:1536 + (tq % 2) * 512]
                    if tq < 2 else
                    EXPS[:, 2, (tq - 2) * 512:(tq - 1) * 512])
            rq_s.append(rs_s)
            em.op("scalar", "activation", ROW3, PA[0:1, qs], AF.Ln,
                  wait=(ss2,), inc=None)
            t = sS.tick()
            em.op("scalar", "activation", VSTG[0:1, tq, :], ROW3, AF.Exp,
                  scale=-0.5, wait=(), inc=t)
            rq_rs.append(t)
            ps_read[tq] = t

        fin_c = [None] * 8
        t_eq = [None] * 4
        kv_c = [None] * 8
        bcq = SCRB

        def pre_v(ec):
            if ec == 0:
                for tq in range(4):
                    qs = slice(tq * 512, (tq + 1) * 512)
                    t_b = sT.tick()
                    em.op("tensor", "matmul", PB[:, qs],
                          ONESP[:], VSTG[:, tq, :],
                          start=True, stop=True,
                          wait=(rq_rs[tq], ps_read[4 + tq]), inc=t_b)
                    t_ev = sV.tick()
                    em.op("vector", "tensor_copy", rq_s[tq], PB[:, qs],
                          wait=(t_b, kn_tok), inc=t_ev)
                    t_eq[tq] = t_ev
                    ps_read[4 + tq] = t_ev

        def ev_v(ec, tq, pt, mmtok):
            qs = slice(tq * 512, (tq + 1) * 512)
            tok = sV.tick()
            em.op("vector", "tensor_copy", X2[:, qs], pt,
                  wait=(mmtok,), inc=tok)
            if tq == 3:
                em.op("vector", "scalar_tensor_tensor", X2,
                      X2, CC[:], B2[:, ec, :],
                      ALU.mult, ALU.mult, accum_out=GR[:, ec:ec + 1],
                      wait=(kn_ec[ec],), inc=None)
                for tq2 in range(4):
                    qs2 = slice(tq2 * 512, (tq2 + 1) * 512)
                    em.op("vector", "scalar_tensor_tensor", bcq[:, 0:512],
                          B1[:, ec, qs2], GR[:, ec:ec + 1], rq_s[tq2],
                          ALU.mult, ALU.mult, wait=(t_eq[tq2],), inc=None)
                    tok = sV.tick()
                    em.op("vector", "tensor_add", X[:, ec, qs2],
                          X[:, ec, qs2], bcq[:, 0:512], wait=(), inc=tok)
                fin_c[ec] = tok
                kv_c[ec] = tok
            return tok
        proj("aWv", XN, ev_v, pre_ec=pre_v)
        ln_bc_free[0] = fin_c[7]
        state["x_ready"] = fin_c

    def ffn():
        mix_dq = {}

        def mix_hook(tq, tok):
            lo = tq * 512
            hi = (tq + 1) * 512
            s = 1 if tq == 0 else 0
            for dc in range(8):
                sh1 = XN[:, dc, lo + s:hi]
                sh0 = XN[:, dc, lo + s - 1:hi - 1]
                em.op("vector", "tensor_sub", X2[:, lo + s:hi], sh1, sh0,
                      wait=(tok,), inc=None)
                em.op("vector", "scalar_tensor_tensor", B1[:, dc, lo + s:hi],
                      X2[:, lo + s:hi], MUR[:, dc:dc + 1], sh0,
                      ALU.mult, ALU.add, wait=(), inc=None)
                t2 = sV.tick()
                em.op("vector", "scalar_tensor_tensor", B2[:, dc, lo + s:hi],
                      X2[:, lo + s:hi], MUK[:, dc:dc + 1], sh0,
                      ALU.mult, ALU.add, wait=(), inc=t2)
                if tq == 0:
                    em.op("vector", "tensor_scalar", B1[:, dc, 0:1],
                          XN[:, dc, 0:1], MUR[:, dc:dc + 1], None, ALU.mult,
                          wait=(), inc=None)
                    t2 = sV.tick()
                    em.op("vector", "tensor_scalar", B2[:, dc, 0:1],
                          XN[:, dc, 0:1], MUK[:, dc:dc + 1], None, ALU.mult,
                          wait=(), inc=t2)
                mix_dq[(dc, tq)] = t2
            return t2

        mix_q, _ = layer_norm(state["x_ready"], qhook=mix_hook)

        def ev_r(ec, tq, pt, mmtok):
            tok = sS.tick()
            em.op("scalar", "activation",
                  XN[:, ec, tq * 512:(tq + 1) * 512], pt, AF.Sigmoid,
                  wait=(mmtok, mix_dq[(ec, 3)]), inc=tok)
            return tok
        PHASE_MARKS.append(("ffn:rp", sT.n, sV.n, sS.n))
        r_last_mm = proj("fWr", B1, ev_r, rhs_q=mix_q, rhs_dq=mix_dq)

        prev_sq = [None]
        kk_dq = {}

        def ev_kk(ec, tq, pt, mmtok):
            qs = slice(tq * 512, (tq + 1) * 512)
            t1 = sS.tick()
            em.op("scalar", "activation", X2[:, qs], pt, AF.Relu,
                  wait=(mmtok, prev_sq[0]), inc=t1)
            t2 = sV.tick()
            em.op("vector", "tensor_mul", B1[:, ec, qs], X2[:, qs], X2[:, qs],
                  wait=(t1, r_last_mm), inc=t2)
            prev_sq[0] = t2
            kk_dq[(ec, tq)] = t2
            return t2
        proj("fWk", B2, ev_kk, rhs_q=mix_q, rhs_dq=mix_dq)
        PHASE_MARKS.append(("ffn:vp", sT.n, sV.n, sS.n))

        fin_c = [None] * 8

        def ev_vv(ec, tq, pt, mmtok):
            qs = slice(tq * 512, (tq + 1) * 512)
            em.op("vector", "scalar_tensor_tensor", SCR[:], pt,
                  C1F[:], XN[:, ec, qs], ALU.mult, ALU.mult,
                  wait=(mmtok,), inc=None)
            tok = sV.tick()
            em.op("vector", "tensor_add", X[:, ec, qs], X[:, ec, qs], SCR[:],
                  wait=(), inc=tok)
            fin_c[ec] = tok
            return tok
        proj("fWv", B1, ev_vv, rhs_dq=kk_dq)
        state["x_ready"] = fin_c

    def attn_b():
        xn_q, xn_dq = layer_norm(state["x_ready"])
        xn_tok = xn_q[3]

        PHASE_MARKS.append(("attn_b:vproj", sT.n, sV.n, sS.n))
        wv = wdr["bVw"]
        vgate = sT.now()
        mm_toks = {}
        vst_free = [None] * 4
        vd_tok = None
        nstg = 0
        for grp in range(2):
            for hpq in range(2):
                hs = slice(hpq * 512, (hpq + 1) * 512)
                pts = list(range(8))
                for dc in range(8):
                    s = dc % 4
                    g = grp * 2 + hpq
                    lw = [mm_toks.get((g, dc - 4)) if dc >= 4 else vgate]
                    ltok = sG.tick(16)
                    em.op("gpsimd", "dma_start",
                          out=WBUFf[:, s * 512:(s + 1) * 512],
                          in_=wv[dc * 128:(dc + 1) * 128, hs],
                          wait=lw, inc=ltok)
                    for i, sc in enumerate(range(grp * 8, grp * 8 + 8)):
                        pi = pts[i]
                        w = [ltok]
                        if dc == 0:
                            w.append(ps_read[pi])
                            if hpq == 0 and i == 0:
                                w.append(xn_q[grp * 2 + (i >= 4)])
                        tok = sT.tick() if i == 7 else None
                        em.op("tensor", "matmul", pslot_ap(pi),
                              XN[:, dc, sc * 128:(sc + 1) * 128],
                              WBUFf[:, s * 512:(s + 1) * 512],
                              start=(dc == 0), stop=False,
                              wait=w, inc=tok)
                    mm_toks[(g, dc)] = tok
                vgate = mm_toks[(g, 7)]
                for i, sc in enumerate(range(grp * 8, grp * 8 + 8)):
                    pi = pts[i]
                    mt = sT.tick()
                    em.op("tensor", "matmul", pslot_ap(pi), ONESR[0:1, 0:128],
                          VBR[0:1, hs], start=False, stop=True,
                          wait=(t_vbr, t_const), inc=mt)
                    vgate = mt
                    par = nstg % 4
                    nstg += 1
                    vt = sV.tick()
                    em.op("vector", "tensor_copy", VSTG[:, 6 + par, :],
                          pslot_ap(pi), wait=(mt, vst_free[par]), inc=vt)
                    ps_read[pi] = vt
                    dt = sR.tick(16)
                    em.op("sync", "dma_start",
                          out=vdram[sc * 128:(sc + 1) * 128, hs],
                          in_=VSTG[:, 6 + par, :], wait=(vt,), inc=dt)
                    vst_free[par] = dt
                    vd_tok = dt
                wslot_free[0] = vgate
        wslot_free[1] = vgate
        v_done = vd_tok

        PHASE_MARKS.append(("attn_b:loop", sT.n, sV.n, sS.n))
        def ev_bq(ec, tq, pt, mmtok):
            tok = sS.tick()
            em.op("scalar", "activation",
                  B1[:, ec, tq * 512:(tq + 1) * 512], pt, AF.Copy,
                  wait=(mmtok,), inc=tok)
            return tok
        proj("bQw", XN, ev_bq, rhs_q=xn_q, rhs_dq=xn_dq,
             stationary_extra=(
                 lambda ec, tq: QBR[0:1, ec * 128:(ec + 1) * 128],
                 lambda ec, tq: ONESR[0:1, 0:512], t_vbr))

        def ev_bk(ec, tq, pt, mmtok):
            tok = sS.tick()
            em.op("scalar", "activation",
                  B2[:, ec, tq * 512:(tq + 1) * 512], pt, AF.Copy,
                  wait=(mmtok,), inc=tok)
            return tok
        proj("bKw", XN, ev_bk, rhs_q=xn_q, rhs_dq=xn_dq,
             stationary_extra=(
                 lambda ec, tq: KBR[0:1, ec * 128:(ec + 1) * 128],
                 lambda ec, tq: ONESR[0:1, 0:512], t_vbr))

        first_w = [t for t in ps_read[0:4] if t is not None]
        pb_free = [t for t in ps_read[4:8] if t is not None]
        kb_done = sS.now()
        exp_tok = {}
        sc_tok = {}
        ctx_tok = {}
        vsc_tok = {}
        z_tok = {}
        sc_last = {}
        ktp_tok = {}
        evac = {"tok": None}
        NIT = NH * NKT

        def prep_ktp(h):
            half_p = (h % 2) * 64
            s = h % 2
            w = [kb_done, sc_last.get(h - 2)]
            em.op("vector", "memset", KTP[64 - half_p:128 - half_p, s, :], 0.0,
                  wait=w, inc=None)
            tok = sV.tick()
            em.op("vector", "tensor_copy", KTP[half_p:half_p + 64, s, :],
                  B2[half_p:half_p + 64, h // 2, :], wait=(), inc=tok)
            ktp_tok[h] = tok

        def emit_scores(it, qh):
            par3 = it % 3
            h = it // NKT
            kt = it % NKT
            chunk = h // 2
            if qh == 0:
                if kt == 0:
                    if h == 0:
                        prep_ktp(0)
                    if h + 1 < NH:
                        prep_ktp(h + 1)
                vr = sR.tick(16)
                em.op("sync", "dma_start", out=VRING[:, it % 6, :],
                      in_=vdram[kt * 128:(kt + 1) * 128,
                                h * HP:(h + 1) * HP],
                      wait=(v_done, vsc_tok.get(it - 6)), inc=vr)
                sc_state[it] = {"vr": vr}
            for qq in (2 * qh, 2 * qh + 1):
                w = [exp_tok.get((it - 1, qh))]
                if qq == 2 * qh:
                    w.append(ktp_tok[h])
                    if it == 0:
                        w += first_w
                tok = sT.tick() if qq == 2 * qh + 1 else None
                em.op("tensor", "matmul", PA[:, qq * 512:(qq + 1) * 512],
                      KTP[:, h % 2, kt * 128:(kt + 1) * 128],
                      B1[:, chunk, qq * 512:(qq + 1) * 512],
                      start=True, stop=True, wait=w, inc=tok)
            sc_tok[(it, qh)] = tok
            hs = slice(qh * 1024, (qh + 1) * 1024)
            par = it % 4
            et = sS.tick()
            em.op("scalar", "activation",
                  EXPS[:, par3, hs], PA[:, hs],
                  AF.Exp, bias=BN16[:], scale=0.125,
                  accum_out=ZH[:, par, qh:qh + 1],
                  wait=(tok, t_const, ctx_tok.get(it - 3)), inc=et)
            exp_tok[(it, qh)] = et
            if qh == 1:
                sc_last[h] = tok

        def emit_scores_tail(it):
            par = it % 4
            par3 = it % 3
            h = it // NKT
            kt = it % NKT
            half_p = (h % 2) * 64
            vr = sc_state.pop(it)["vr"]
            t_za = sV.tick()
            em.op("vector", "tensor_add", ZS[:, par:par + 1],
                  ZH[:, par, 0:1], ZH[:, par, 1:2],
                  wait=(exp_tok[(it, 0)], exp_tok[(it, 1)]), inc=t_za)
            t_rc = sV.tick()
            em.op("vector", "reciprocal", RZ[:, par:par + 1],
                  ZS[:, par:par + 1], wait=(t_za,), inc=t_rc)
            if kt < 6:
                em.op("vector", "memset",
                      VSC[:, it % 6, 64 - half_p:128 - half_p],
                      0.0, wait=(ctx_tok.get(it - 6),), inc=None)
            vt = sV.tick()
            em.op("vector", "tensor_scalar",
                  VSC[:, it % 6, half_p:half_p + 64],
                  VRING[:, it % 6, :], RZ[:, par:par + 1], None, ALU.mult,
                  wait=(vr, t_rc, ctx_tok.get(it - 6)), inc=vt)
            vsc_tok[it] = vt

        sc_state = {}
        def emit_ctx_half(it, half):
            par = it % 4
            par3 = it % 3
            h = it // NKT
            kt = it % NKT
            chunk = h // 2
            tok = None
            for qq in (2 * half, 2 * half + 1):
                w = [vsc_tok[it]]
                if kt == 0 and qq == 0:
                    w.append(evac["tok"])
                    if it <= 1:
                        w += pb_free
                tok = sT.tick() if qq == 3 else None
                em.op("tensor", "matmul",
                      PB[:, qq * 512:(qq + 1) * 512],
                      VSC[:, it % 6, :],
                      EXPS[:, par3, qq * 512:(qq + 1) * 512],
                      start=(kt == 0 and h % 2 == 0),
                      stop=(kt == NKT - 1 and h % 2 == 1),
                      wait=w, inc=tok)
            if qq == 3:
                ctx_tok[it] = tok
                if kt == NKT - 1 and h % 2 == 1:
                    t = sV.tick()
                    em.op("vector", "tensor_copy", XN[:, chunk, :],
                          PB[:, 0:T], wait=(tok,), inc=t)
                    evac["tok"] = t

        for it in range(NIT + 2):
            if it < NIT:
                emit_scores(it, 0)
                emit_scores(it, 1)
                emit_scores_tail(it)
            if it >= 2:
                emit_ctx_half(it - 2, 0)
                emit_ctx_half(it - 2, 1)
        evac_tok = evac["tok"]
        for i in range(8):
            ps_read[i] = exp_tok[(NIT - 1, i // 2)] if i < 4 else evac_tok

        PHASE_MARKS.append(("attn_b:outproj", sT.n, sV.n, sS.n))
        fin_c = [None] * 8

        def ev_o(ec, tq, pt, mmtok):
            qs = slice(tq * 512, (tq + 1) * 512)
            tok = sV.tick()
            em.op("vector", "scalar_tensor_tensor", X[:, ec, qs], pt,
                  LB[:, ec:ec + 1], X[:, ec, qs], ALU.add, ALU.add,
                  wait=(mmtok, t_small), inc=tok)
            fin_c[ec] = tok
            return tok
        proj("bLw", XN, ev_o, rhs_ready=evac_tok)
        state["x_ready"] = fin_c

    subs = [attn_a, ffn, attn_b, ffn]
    names = ["attn_a", "ffn1", "attn_b", "ffn2"]
    PHASE_MARKS.clear()
    for nm, fsub in zip(names[:N_SUB], subs[:N_SUB]):
        PHASE_MARKS.append((nm + ":start", sT.n, sV.n, sS.n))
        fsub()
    PHASE_MARKS.append(("end", sT.n, sV.n, sS.n))

    xr_fin = state["x_ready"]
    if not isinstance(xr_fin, list):
        xr_fin = [xr_fin] * 8
    out_toks = []
    for dc in range(8):
        tok = sQ.tick(16)
        em.op("sync", "dma_start", out=y[dc * 128:(dc + 1) * 128, :],
              in_=X[:, dc, :], wait=(xr_fin[dc],), inc=tok)
        out_toks.append(tok)
    if DBG:
        dXN = nc.dram_tensor("dXN", [D, T], BF16, kind="ExternalOutput")
        dB1 = nc.dram_tensor("dB1", [D, T], BF16, kind="ExternalOutput")
        dB2 = nc.dram_tensor("dB2", [D, T], BF16, kind="ExternalOutput")
        dGR = nc.dram_tensor("dGR", [128, 8], F32, kind="ExternalOutput")
        for dc in range(8):
            em.op("sync", "dma_start", out=dXN[dc * 128:(dc + 1) * 128, :],
                  in_=XN[:, dc, :], wait=(state["x_ready"],), inc=sQ.tick(16))
            em.op("sync", "dma_start", out=dB1[dc * 128:(dc + 1) * 128, :],
                  in_=B1[:, dc, :], wait=(), inc=sQ.tick(16))
            em.op("sync", "dma_start", out=dB2[dc * 128:(dc + 1) * 128, :],
                  in_=B2[:, dc, :], wait=(), inc=sQ.tick(16))
        out_toks.append(sQ.tick(16))
        em.op("sync", "dma_start", out=dGR[:], in_=GR[:], wait=(),
              inc=out_toks[-1])
    em.wait("sync", out_toks[-1])

    with nc.Block() as block:
        @block.sync
        def _(e):
            seen = {}
            for f in em.q["sync"]:
                f(e, seen)

        @block.gpsimd
        def _(e):
            seen = {}
            for f in em.q["gpsimd"]:
                f(e, seen)

        @block.tensor
        def _(e):
            seen = {}
            for f in em.q["tensor"]:
                f(e, seen)

        @block.vector
        def _(e):
            seen = {}
            for f in em.q["vector"]:
                f(e, seen)

        @block.scalar
        def _(e):
            seen = {}
            for f in em.q["scalar"]:
                f(e, seen)

    stack.close()
    return nc


_NC_CACHE = {}


def _get_nc():
    if "nc" not in _NC_CACHE:
        _NC_CACHE["nc"] = build_nc()
    return _NC_CACHE["nc"]


def kernel(**inputs):
    f = lambda k: np.ascontiguousarray(np.asarray(inputs[k], dtype=np.float32))
    x = f("L0_attn_a_x")
    tr = lambda a: np.ascontiguousarray(a.T)

    wm = {
        "aWq": tr(f("L0_attn_a_W_q")),
        "aWk": tr(f("L0_attn_a_W_k")),
        "aWv": tr(f("L0_attn_a_W_v")),
        "fWr": tr(f("L0_ffn_W_r")),
        "fWk": tr(f("L0_ffn_W_k")),
        "fWv": tr(f("L0_ffn_W_v")),
        "bQw": tr(f("L0_attn_b_q_w").reshape(D, D)),
        "bKw": tr(f("L0_attn_b_k_w").reshape(D, D)),
        "bVw": tr(f("L0_attn_b_v_w").reshape(D, D)),
        "bLw": tr(f("L0_attn_b_l_w")),
        "qbr": f("L0_attn_b_q_b").reshape(1, D),
        "kbr": f("L0_attn_b_k_b").reshape(1, D),
        "lb": np.ascontiguousarray(f("L0_attn_b_l_b").reshape(8, 128).T),
        "mur": np.ascontiguousarray(f("L0_ffn_mu_r").reshape(8, 128).T),
        "muk": np.ascontiguousarray(f("L0_ffn_mu_k").reshape(8, 128).T),
        "vbr": f("L0_attn_b_v_b").reshape(1, D),
    }

    nc = _get_nc()
    in_maps = []
    for b in range(8):
        m = dict(wm)
        m["xt"] = np.ascontiguousarray(x[b].T)
        in_maps.append(m)
    res = run_bass_kernel_spmd(nc, in_maps, core_ids=list(range(8)))
    out = np.stack([np.asarray(res.results[b]["y"]).T for b in range(8)])
    return np.ascontiguousarray(out.astype(np.float32))
